# version 08
# baseline (speedup 1.0000x reference)
import contextlib
import math

import numpy as np
import ml_dtypes

import concourse.bass as bass
import concourse.mybir as mybir
from concourse.bass_utils import run_bass_kernel_spmd

F32 = mybir.dt.float32
BF16 = mybir.dt.bfloat16
I32 = mybir.dt.int32
AF = mybir.ActivationFunctionType
OP = mybir.AluOpType
AX = mybir.AxisListType

D = 2048
HID = 5632
NH = 16
DH = 128
NDC = D // 128
NHC = HID // 128
ISCALE = 1.0 / math.sqrt(DH)
NEG = -1.0e9

T = 8192
T4 = 4096
OWN = 1024
Q4 = 2048
NQL = T4 // Q4
NKC = T // 128
NCORE = 8

BF = ml_dtypes.bfloat16


class Ctr:
    def __init__(self, sem, step=1):
        self.sem = sem
        self.step = step
        self.v = 0

    def inc(self, instr, n=None):
        n = self.step if n is None else n
        instr.then_inc(self.sem, n)
        self.v += n
        return self.v



def dram3(ap2d, p=128):
    return ap2d.rearrange("(c p) t -> p c t", p=p)


def sb3(ap2d, c):
    return ap2d.rearrange("p (c t) -> p c t", c=c)


class Builder:
    _uid = 0

    @classmethod
    def _un(cls, n):
        cls._uid += 1
        return f"{n}_{cls._uid}"

    def __init__(self, debug=False):
        self.debug = debug
        nc = bass.Bass("TRN2", target_bir_lowering=False, debug=False)
        self.nc = nc
        ap = lambda *a, **k: nc.dram_tensor(*a, **k).ap()
        self.xT = ap("xT", [D, T4], F32, kind="ExternalInput")
        self.noncex = ap("noncex", [1, 1], I32, kind="ExternalInput")
        self.wqg = ap("wqg", [HID, D], I32, kind="ExternalInput")
        self.wqu = ap("wqu", [HID, D], I32, kind="ExternalInput")
        self.wqd = ap("wqd", [D, HID], I32, kind="ExternalInput")
        self.sgx = ap("sgx", [HID, 1], F32, kind="ExternalInput")
        self.sux = ap("sux", [HID, 1], F32, kind="ExternalInput")
        self.sdx = ap("sdx", [D, 1], F32, kind="ExternalInput")
        self.wqkv = ap("wqkv", [D, 3 * D], F32, kind="ExternalInput")
        self.bqkv = ap("bqkv", [3 * D, 1], F32, kind="ExternalInput")
        self.wout = ap("wout", [D, D], F32, kind="ExternalInput")
        self.boutx = ap("boutx", [D, 1], F32, kind="ExternalInput")
        self.maskrow = ap("maskrow", [1, NKC], F32, kind="ExternalInput")
        self.diagmask = ap("diagmask", [8 * 128, OWN], F32, kind="ExternalInput")
        self.identin = ap("identin", [128, 128], BF16, kind="ExternalInput")
        self.onesin = ap("onesin", [128, 1], BF16, kind="ExternalInput")
        self.onesk1 = ap("onesk1", [1, 128], BF16, kind="ExternalInput")

        self.y3T = ap("y3T", [D, OWN], F32, kind="ExternalOutput")

        self.WgB = ap("WgB", [HID, D], BF16)
        self.WuB = ap("WuB", [HID, D], BF16)
        self.WdB = ap("WdB", [D, HID], BF16)
        self.HT1 = ap("HT1", [HID, Q4], BF16)
        self.KT = ap("KT", [D, T4], BF16)
        self.VT = ap("VT", [NH * T4, DH], BF16)
        self.Y1OWN = ap("Y1OWN", [D, OWN], F32)
        self.Y1OTH = ap("Y1OTH", [D, T4 - OWN], BF16)
        self.Y2 = ap("Y2", [D, OWN], F32)
        self.SHK = [ap(f"SHK{s}", [D, T4], BF16, addr_space="Shared") for s in range(2)]
        self.SHV = [ap(f"SHV{s}", [NH * T4, DH], BF16, addr_space="Shared") for s in range(2)]
        self.SHKM = [ap(f"SHKM{s}", [1, NH], F32, addr_space="Shared") for s in range(2)]
        self.SHFLG = [ap(f"SHFLG{s}", [1, 2], I32, addr_space="Shared") for s in range(2)]

        self.dbg = {}
        if debug:
            self.dbg["dKT"] = ap("dKT", [D, T4], BF16, kind="ExternalOutput")
            self.dbg["dVT"] = ap("dVT", [NH * T4, DH], BF16, kind="ExternalOutput")
            self.dbg["dY1"] = ap("dY1", [D, OWN], F32, kind="ExternalOutput")
            self.dbg["dY1O"] = ap("dY1O", [D, T4 - OWN], BF16, kind="ExternalOutput")
            self.dbg["dQT"] = ap("dQT", [128, NH * OWN], BF16, kind="ExternalOutput")
            self.dbg["dAT"] = ap("dAT", [128, NDC * OWN], BF16, kind="ExternalOutput")
            self.dbg["dY2"] = ap("dY2", [D, OWN], F32, kind="ExternalOutput")
            self.dbg["dNB"] = ap("dNB", [1, NH], F32, kind="ExternalOutput")
            self.dbg["dNB2"] = ap("dNB2", [3, NH], F32, kind="ExternalOutput")
            self.dbg["dSHKM"] = ap("dSHKM", [2, NH], F32, kind="ExternalOutput")
            self.dbg["dSHFL"] = ap("dSHFL", [2, 2], I32, kind="ExternalOutput")
            self.dbg["dPOLL"] = ap("dPOLL", [1, 2], I32, kind="ExternalOutput")
            self.dbg["dHT"] = ap("dHT", [HID, Q4], BF16, kind="ExternalOutput")
            self.dbg["dXN1"] = ap("dXN1", [D, T4], BF16, kind="ExternalOutput")

    def build(self):
        nc = self.nc
        with contextlib.ExitStack() as top:
            sb = lambda n, s, d: top.enter_context(nc.sbuf_tensor(self._un(n), s, d))
            self.ident = sb("ident", [128, 128], BF16)
            self.ones = sb("ones", [128, 1], BF16)
            self.o1x128 = sb("o1x128", [1, 128], BF16)
            self.sgv = sb("sgv", [128, NHC], F32)
            self.suv = sb("suv", [128, NHC], F32)
            self.sdv = sb("sdv", [128, NDC], F32)
            self.bqv = sb("bqv", [128, 3 * NDC], F32)
            self.bov = sb("bov", [128, NDC], F32)
            self.k2m = sb("k2m", [1, NH], F32)
            self.pkm = sb("pkm", [1, NH], F32)
            self.snon = sb("snon", [1, 1], I32)
            self.polldbg = sb("polldbg", [1, 2], I32)
            self.q2m = sb("q2m", [1, NH], F32)
            self.nbh = sb("nbh", [1, NH], F32)
            self.biasT = sb("biasT", [128, NH * (NKC + 1)], F32)
            self.mrow = sb("mrow", [1, NKC], F32)
            self.cv_stack = top.enter_context(contextlib.ExitStack())
            csb = lambda n, s, d: self.cv_stack.enter_context(
                nc.sbuf_tensor(self._un(n), s, d))
            self.cv_i = [csb(f"cvi{i}", [128, 512], I32) for i in range(4)]
            self.cv_f = [csb(f"cvf{i}", [128, 1024], F32) for i in range(2)]
            self.cv_o = [csb(f"cvo{i}", [128, 1024], BF16) for i in range(4)]
            self.ps = top.enter_context(nc.psum_tensor("ps", [128, 4096], F32))

            mk = lambda n, s=1: Ctr(top.enter_context(nc.semaphore(n)), s)
            self.dS = mk("dS", 16)
            self.dW = mk("dW", 16)
            self.dH = mk("dH", 16)
            self.dX = mk("dX", 16)
            self.dKV = mk("dKV", 16)
            self.dG = mk("dG", 16)
            self.dC = mk("dC", 16)
            self.dHt = mk("dHt", 16)
            self.dKo = mk("dKo", 16)
            self.dQi = [top.enter_context(nc.semaphore(f"dQi{k}")) for k in range(4)]
            self.dQf = [top.enter_context(nc.semaphore(f"dQf{k}")) for k in range(2)]
            self.eT = mk("eT")
            self.eV = mk("eV")
            self.eA = mk("eA")
            self.eG = mk("eG")

            self.block = top.enter_context(nc.Block())

            self.p_consts()
            self.p_wconv()
            self.p_ffn1()
            def _f(e):
                e.wait_ge(self.eG.sem, self.eG.v)
                e.wait_ge(self.dS.sem, self.dS.v)
                e.wait_ge(self.dG.sem, self.dG.v)
            for eng in ("sync", "gpsimd", "vector", "scalar", "tensor"):
                getattr(self.block, eng)(_f)
            self.cv_stack.close()
            with contextlib.ExitStack() as mid:
                msb = lambda n, s, d: mid.enter_context(nc.sbuf_tensor(self._un(n), s, d))
                self.QT = msb("QT", [128, NH * OWN], BF16)
                self.p_qkv()
                self.kv_ready_dKo = self.dKo.v
                self.p_norms()
                self.p_exchange()
                self.p_bias()
                self.AT = msb("AT", [128, NDC * OWN], BF16)
                self.p_attention()
                self.p_outproj()
            self.p_ffn3()
            self.p_final()
        return nc

    def bank(self, b, w=512, p=128):
        return self.ps[0:p, 512 * b:512 * b + w]

    def fence(self, light=False):
        if light:
            cs = [self.eT, self.eV, self.eA]
        else:
            cs = [self.eT, self.eV, self.eA,
                  self.dW, self.dH, self.dX, self.dKV, self.dG,
                  self.dC, self.dHt, self.dKo]
        vals = [(c.sem, c.v) for c in cs]
        for eng in ("sync", "tensor", "vector", "scalar", "gpsimd"):
            def _f(e, vals=vals):
                for sem, v in vals:
                    if v:
                        e.wait_ge(sem, v)
            getattr(self.block, eng)(_f)

    def p_consts(self):
        dS, eV = self.dS, self.eV

        def _s(s):
            for dst, src in [(self.ident, self.identin), (self.ones, self.onesin),
                             (self.o1x128, self.onesk1), (self.mrow, self.maskrow),
                             (self.snon, self.noncex)]:
                dS.inc(s.dma_start(out=dst[:, :], in_=src[:, :]))
            for dst, srcp in [
                (self.sgv, self.sgx), (self.suv, self.sux), (self.sdv, self.sdx),
                (self.bqv, self.bqkv), (self.bov, self.boutx),
            ]:
                nch = dst.shape[1]
                dS.inc(s.dma_start(out=sb3(dst[:, :], nch), in_=dram3(srcp),
                                   allow_slow_non_contiguous=True))
        self.block.sync(_s)

        def _v(v):
            v.wait_ge(dS.sem, dS.v)
            v.tensor_scalar_mul(self.sdv[:, :], self.sdv[:, :], 0.5)
            v.memset(self.k2m[:, :], 0.0)
            eV.inc(v.memset(self.q2m[:, :], 0.0))
        self.block.vector(_v)

    def p_wconv(self):
        dS, dG, eG = self.dS, self.dG, self.eG
        jobs = []
        for hc in range(NHC):
            for c0 in (0, 512, 1024, 1536):
                jobs.append((self.wqg, self.WgB, hc * 128, c0, 512, I32))
                jobs.append((self.wqu, self.WuB, hc * 128, c0, 512, I32))
        n_gu = len(jobs)
        for dc in range(NDC):
            for c0 in range(0, HID, 512):
                jobs.append((self.wqd, self.WdB, dc * 128, c0, 512, I32))
        n_wqd = len(jobs)

        in_slot = []
        dg_after = []
        eG0 = eG.v
        nj = len(jobs)
        ci = [0, 0, 0, 0]
        cf = [0, 0]

        def _s(s):
            nf = 0
            for ji, (srcp, dst, row, c0, w, dt) in enumerate(jobs):
                if dt == I32:
                    if ji >= 4:
                        s.wait_ge(eG.sem, eG0 + ji - 3)
                    stg = self.cv_i[ji % 4]
                    sem = self.dQi[ji % 4]
                    ci[ji % 4] += 16
                    cnt = ci[ji % 4]
                else:
                    if nf >= 2:
                        s.wait_ge(eG.sem, eG0 + ji - 1)
                    stg = self.cv_f[nf % 2]
                    sem = self.dQf[nf % 2]
                    cf[nf % 2] += 16
                    cnt = cf[nf % 2]
                    nf += 1
                s.dma_start(out=stg[:, 0:w],
                            in_=srcp[row:row + 128, c0:c0 + w]).then_inc(sem, 16)
                in_slot.append((sem, cnt))
        self.block.sync(_s)

        def _g(g):
            for ji, (srcp, dst, row, c0, w, dt) in enumerate(jobs):
                g.wait_ge(eG.sem, eG0 + ji + 1)
                dG.inc(g.dma_start(out=dst[row:row + 128, c0:c0 + w],
                                   in_=self.cv_o[ji % 4][:, 0:w]))
                dg_after.append(dG.v)
        self.block.gpsimd(_g)

        def _v(v):
            nf = 0
            for ji, (srcp, dst, row, c0, w, dt) in enumerate(jobs):
                sem, cnt = in_slot[ji]
                v.wait_ge(sem, cnt)
                if ji >= 4:
                    v.wait_ge(dG.sem, dg_after[ji - 4])
                if dt == I32:
                    stg = self.cv_i[ji % 4]
                else:
                    stg = self.cv_f[nf % 2]
                    nf += 1
                if dt == I32:
                    eG.inc(v.tensor_scalar(out=self.cv_o[ji % 4][:, 0:w],
                                           in0=stg[:, 0:w], scalar1=128.0,
                                           scalar2=None, op0=OP.subtract))
                else:
                    eG.inc(v.tensor_copy(out=self.cv_o[ji % 4][:, 0:w],
                                         in_=stg[:, 0:w]))
        self.block.vector(_v)

        self.wcv_slab = {hs: dg_after[32 * (hs + 1) - 1] for hs in range(NHC // 4)}
        self.wcv_wqd = dg_after[n_wqd - 1]
        self.wconv_dG = dG.v

    def mk_ln(self, ph, tw=1024):
        sb = lambda n, s, d: ph.enter_context(self.nc.sbuf_tensor(self._un(n), s, d))
        return dict(
            tw=tw,
            xb=sb("ln_xb", [128, NDC * tw], BF16),
            sq=sb("ln_sq", [128, NDC * tw], BF16),
            m=sb("ln_m", [1, tw], F32),
            r=sb("ln_r", [1, tw], F32),
            t1=sb("ln_t1", [1, tw], F32),
            mb=sb("ln_mb", [1, tw], BF16),
            rb=sb("ln_rb", [1, tw], BF16),
            reb=sb("ln_reb", [1, tw], BF16),
            msb=sb("ln_msb", [128, tw], BF16),
            rsb=sb("ln_rsb", [128, tw], BF16),
        )

    def emit_ln(self, dst, dst_base, dst_stride, ln, tw=1024):
        blk = self.block
        eT, eV, eA = self.eT, self.eV, self.eA
        n = NDC
        xb = ln["xb"]
        nh = tw // 512

        def _a(a):
            a.wait_ge(self.dX.sem, self.dX.v)
            a.wait_ge(self.dC.sem, self.dC.v)
            eA.inc(a.activation(out=ln["sq"][:, 0:n * tw], in_=xb[:, 0:n * tw],
                                func=AF.Square))
        blk.scalar(_a)

        def _t(t):
            t.wait_ge(eA.sem, eA.v)
            t.wait_ge(eV.sem, eV.v)
            for dc in range(n):
                for h in range(nh):
                    t.matmul(self.ps[0:1, h * 512:(h + 1) * 512], self.ones[:, :],
                             xb[:, dc * tw + h * 512:dc * tw + (h + 1) * 512],
                             start=(dc == 0), stop=(dc == n - 1))
            ins = None
            for dc in range(n):
                for h in range(nh):
                    ins = t.matmul(self.ps[0:1, 1024 + h * 512:1024 + (h + 1) * 512],
                                   self.ones[:, :],
                                   ln["sq"][:, dc * tw + h * 512:dc * tw + (h + 1) * 512],
                                   start=(dc == 0), stop=(dc == n - 1))
            eT.inc(ins)
        blk.tensor(_t)

        def _v(v):
            v.wait_ge(eT.sem, eT.v)
            v.tensor_scalar_mul(ln["m"][0:1, 0:tw], self.ps[0:1, 0:tw], 1.0 / D)
            v.tensor_scalar_mul(ln["r"][0:1, 0:tw], self.ps[0:1, 1024:1024 + tw], 1.0 / D)
            v.tensor_tensor(ln["t1"][0:1, 0:tw], ln["m"][0:1, 0:tw],
                            ln["m"][0:1, 0:tw], op=OP.mult)
            v.tensor_tensor(ln["r"][0:1, 0:tw], ln["r"][0:1, 0:tw],
                            ln["t1"][0:1, 0:tw], op=OP.subtract)
            eV.inc(v.tensor_scalar_add(ln["r"][0:1, 0:tw], ln["r"][0:1, 0:tw], 1e-5))
        blk.vector(_v)

        def _a(a):
            a.wait_ge(eV.sem, eV.v)
            eA.inc(a.activation(out=ln["t1"][0:1, 0:tw], in_=ln["r"][0:1, 0:tw],
                                func=AF.Sqrt))
        blk.scalar(_a)

        def _v(v):
            v.wait_ge(eA.sem, eA.v)
            v.reciprocal(ln["r"][0:1, 0:tw], ln["t1"][0:1, 0:tw])
            v.tensor_copy(out=ln["rb"][0:1, 0:tw], in_=ln["r"][0:1, 0:tw])
            v.tensor_copy(out=ln["t1"][0:1, 0:tw], in_=ln["rb"][0:1, 0:tw])
            v.tensor_tensor(ln["t1"][0:1, 0:tw], ln["r"][0:1, 0:tw],
                            ln["t1"][0:1, 0:tw], op=OP.subtract)
            v.tensor_copy(out=ln["reb"][0:1, 0:tw], in_=ln["t1"][0:1, 0:tw])
            eV.inc(v.tensor_copy(out=ln["mb"][0:1, 0:tw], in_=ln["m"][0:1, 0:tw]))
        blk.vector(_v)

        def _t(t):
            t.wait_ge(eV.sem, eV.v)
            ins = None
            for h in range(nh):
                t.matmul(self.ps[:, h * 512:(h + 1) * 512], self.o1x128[:, :],
                         ln["mb"][0:1, h * 512:(h + 1) * 512], start=True, stop=True)
                t.matmul(self.ps[:, 1024 + h * 512:1024 + (h + 1) * 512],
                         self.o1x128[:, :], ln["rb"][0:1, h * 512:(h + 1) * 512],
                         start=True, stop=False)
                ins = t.matmul(self.ps[:, 1024 + h * 512:1024 + (h + 1) * 512],
                               self.o1x128[:, :], ln["reb"][0:1, h * 512:(h + 1) * 512],
                               start=False, stop=True)
            eT.inc(ins)
        blk.tensor(_t)

        def _v(v):
            v.wait_ge(eT.sem, eT.v)
            v.tensor_copy(out=ln["msb"][:, 0:tw], in_=self.ps[:, 0:tw])
            eV.inc(v.tensor_copy(out=ln["rsb"][:, 0:tw], in_=self.ps[:, 1024:1024 + tw]))
        blk.vector(_v)

        def _v(v):
            for dc in range(n):
                v.tensor_tensor(ln["sq"][:, dc * tw:(dc + 1) * tw],
                                xb[:, dc * tw:(dc + 1) * tw],
                                ln["msb"][:, 0:tw], op=OP.subtract)
            ins = None
            for dc in range(n):
                o = dst_base + dc * dst_stride
                ins = v.tensor_tensor(dst[:, o:o + tw],
                                      ln["sq"][:, dc * tw:(dc + 1) * tw],
                                      ln["rsb"][:, 0:tw], op=OP.mult)
            eV.inc(ins)
        blk.vector(_v)

    def p_ffn1(self):
        nc, blk = self.nc, self.block
        dW, dH, dX, dG = self.dW, self.dH, self.dX, self.dG
        eT, eV, eA = self.eT, self.eV, self.eA

        for q in range(NQL):
            if q > 0:
                self.fence()
            with contextlib.ExitStack() as ph:
                sb = lambda n, s, d: ph.enter_context(nc.sbuf_tensor(self._un(n), s, d))
                gph = ph.enter_context(contextlib.ExitStack())
                gsb = lambda n, s, d: gph.enter_context(nc.sbuf_tensor(self._un(n), s, d))
                xn = gsb("xn1", [128, NDC * Q4], BF16)

                with contextlib.ExitStack() as lph:
                    ln = self.mk_ln(lph)
                    for tt2 in range(2):
                        tg = q * Q4 + tt2 * 1024

                        def _g(g, tg=tg):
                            g.wait_ge(eV.sem, eV.v)
                            self.dC.inc(g.dma_start(
                                out=sb3(ln["xb"][:, :], NDC),
                                in_=dram3(self.xT[:, tg:tg + 1024])))
                        blk.gpsimd(_g)
                        self.emit_ln(xn, tt2 * 1024, Q4, ln)

                if self.debug:
                    def _g(g, q=q):
                        g.wait_ge(eV.sem, eV.v)
                        dG.inc(g.dma_start(
                            out=dram3(self.dbg["dXN1"][:, q * Q4:(q + 1) * Q4]),
                            in_=sb3(xn[:, :], NDC)))
                    blk.gpsimd(_g)

                self.fence(light=True)
                wgT = [gsb(f"wgT{i}", [128, NDC * 512], BF16) for i in range(2)]
                wuT = [gsb(f"wuT{i}", [128, NDC * 512], BF16) for i in range(2)]
                gt = [gsb(f"gt{i}", [128, 512], BF16) for i in range(2)]
                ht = [gsb(f"ht{i}", [128, 512], BF16) for i in range(4)]
                ht_dma = [0, 0, 0, 0]
                gidx = 0
                slab_pe_done = {}
                prev_dve = [eV.v, eV.v]
                prev_act = [eA.v, eA.v]
                slab_loaded = {}

                for hs in range(NHC // 4):
                    b = hs % 2

                    def _s(s, hs=hs, b=b):
                        if hs >= 2:
                            s.wait_ge(eT.sem, slab_pe_done[hs - 2])
                        if q == 0:
                            s.wait_ge(dG.sem, self.wcv_slab[hs])
                        for dc in range(NDC):
                            dW.inc(s.dma_start_transpose(
                                out=wgT[b][:, dc * 512:(dc + 1) * 512],
                                in_=self.WgB[hs * 512:(hs + 1) * 512,
                                             dc * 128:(dc + 1) * 128]))
                        for dc in range(NDC):
                            dW.inc(s.dma_start_transpose(
                                out=wuT[b][:, dc * 512:(dc + 1) * 512],
                                in_=self.WuB[hs * 512:(hs + 1) * 512,
                                             dc * 128:(dc + 1) * 128]))
                    blk.sync(_s)
                    slab_loaded[hs] = dW.v

                    for hcw in range(4):
                        hc = hs * 4 + hcw
                        for tt in range(4):
                            par = (hcw * 4 + tt) % 2
                            gb, ub = (4, 5) if par == 0 else (6, 7)

                            def _t(t, b=b, hcw=hcw, tt=tt, par=par, gb=gb, ub=ub,
                                   first=(hcw == 0 and tt == 0), hs=hs):
                                if first:
                                    t.wait_ge(dW.sem, slab_loaded[hs])
                                    if hs == 0:
                                        t.wait_ge(eV.sem, eV.v)
                                t.wait_ge(eV.sem, prev_dve[par])
                                t.wait_ge(eA.sem, prev_act[par])
                                for dc in range(NDC):
                                    t.matmul(self.bank(gb),
                                             wgT[b][:, dc * 512 + hcw * 128:
                                                    dc * 512 + hcw * 128 + 128],
                                             xn[:, dc * Q4 + tt * 512:
                                                dc * Q4 + (tt + 1) * 512],
                                             start=(dc == 0), stop=(dc == NDC - 1))
                                ins = None
                                for dc in range(NDC):
                                    ins = t.matmul(self.bank(ub),
                                                   wuT[b][:, dc * 512 + hcw * 128:
                                                          dc * 512 + hcw * 128 + 128],
                                                   xn[:, dc * Q4 + tt * 512:
                                                      dc * Q4 + (tt + 1) * 512],
                                                   start=(dc == 0), stop=(dc == NDC - 1))
                                eT.inc(ins)
                            blk.tensor(_t)
                            slab_pe_done[hs] = eT.v

                            gpar = gidx % 2
                            hpar = gidx % 4

                            def _a(a, hc=hc, gb=gb, gpar=gpar, gidx=gidx):
                                a.wait_ge(eT.sem, eT.v)
                                if gidx >= 2:
                                    a.wait_ge(eV.sem, prev_dve[(gidx - 2) % 2])
                                eA.inc(a.activation(out=gt[gpar][:, :],
                                                    in_=self.bank(gb),
                                                    func=AF.Silu,
                                                    scale=self.sgv[:, hc:hc + 1]))
                            blk.scalar(_a)
                            prev_act[par] = eA.v

                            def _v(v, hc=hc, ub=ub, gpar=gpar, hpar=hpar):
                                v.wait_ge(eA.sem, eA.v)
                                v.wait_ge(self.dHt.sem, ht_dma[hpar])
                                eV.inc(v.scalar_tensor_tensor(
                                    out=ht[hpar][:, :], in0=self.bank(ub),
                                    scalar=self.suv[:, hc:hc + 1], in1=gt[gpar][:, :],
                                    op0=OP.mult, op1=OP.mult))
                            blk.vector(_v)
                            prev_dve[par] = eV.v

                            def _g(g, hc=hc, tt=tt, hpar=hpar):
                                g.wait_ge(eV.sem, eV.v)
                                self.dHt.inc(g.dma_start(
                                    out=self.HT1[hc * 128:(hc + 1) * 128,
                                                 tt * 512:(tt + 1) * 512],
                                    in_=ht[hpar][:, :]))
                                ht_dma[hpar] = self.dHt.v
                            blk.gpsimd(_g)
                            gidx += 1
                gu_done_T = eT.v
                gu_done_G = self.dHt.v
                if self.debug and q == 0:
                    def _s(s):
                        s.wait_ge(dG.sem, gu_done_G)
                        self.dS.inc(s.dma_start(out=self.dbg["dHT"][:, :],
                                                in_=self.HT1[:, :]))
                    blk.sync(_s)
                gph.close()
                self.fence()

                wdT = sb("wdT", [128, NHC * 1024], BF16)
                hbig = sb("hbig", [128, NHC * 512], BF16)
                xdn = sb("xdn", [128, 8 * 512], F32)
                y1t = sb("y1t", [128, 8 * 512], F32)
                y1tb = sb("y1tb", [128, 8 * 512], BF16)

                for dh in range(2):
                    wd_tick = {}

                    def _s(s, dh=dh):
                        if dh == 0:
                            s.wait_ge(self.dHt.sem, gu_done_G)
                            if q == 0:
                                s.wait_ge(dG.sem, self.wcv_wqd)
                        else:
                            s.wait_ge(eT.sem, eT.v)
                        for hc in range(NHC):
                            for ds2 in range(2):
                                dW.inc(s.dma_start_transpose(
                                    out=wdT[:, hc * 1024 + ds2 * 512:
                                            hc * 1024 + (ds2 + 1) * 512],
                                    in_=self.WdB[dh * 1024 + ds2 * 512:
                                                 dh * 1024 + (ds2 + 1) * 512,
                                                 hc * 128:(hc + 1) * 128]))
                            wd_tick[hc] = dW.v
                    blk.sync(_s)

                    for tt in range(4):
                        h_tick = {}

                        def _s(s, dh=dh, tt=tt):
                            s.wait_ge(eT.sem, eT.v)
                            s.wait_ge(eV.sem, eV.v)
                            for piece in range(4):
                                hc0 = piece * 11
                                dH.inc(s.dma_start(
                                    out=sb3(hbig[:, hc0 * 512:(hc0 + 11) * 512], 11),
                                    in_=dram3(self.HT1[hc0 * 128:(hc0 + 11) * 128,
                                                       tt * 512:(tt + 1) * 512])))
                                h_tick[piece] = dH.v
                            dX.inc(s.dma_start(
                                out=sb3(xdn[:, :], 8),
                                in_=dram3(self.xT[dh * 1024:(dh + 1) * 1024,
                                                  q * Q4 + tt * 512:
                                                  q * Q4 + (tt + 1) * 512])))
                        blk.sync(_s)

                        def _t(t):
                            t.wait_ge(eV.sem, eV.v)
                            ins = None
                            for hc in range(NHC):
                                if hc % 11 == 0:
                                    t.wait_ge(dH.sem, h_tick[hc // 11])
                                t.wait_ge(dW.sem, wd_tick[hc])
                                for dcw in range(8):
                                    ins = t.matmul(
                                        self.bank(dcw),
                                        wdT[:, hc * 1024 + dcw * 128:
                                            hc * 1024 + dcw * 128 + 128],
                                        hbig[:, hc * 512:(hc + 1) * 512],
                                        start=(hc == 0), stop=(hc == NHC - 1))
                            eT.inc(ins)
                        blk.tensor(_t)

                        own = (q == 0 and tt < 2)

                        def _v(v, own=own):
                            v.wait_ge(eT.sem, eT.v)
                            v.wait_ge(self.dX.sem, self.dX.v)
                            v.wait_ge(dG.sem, dG.v)
                            dst = y1t if own else y1tb
                            ins = None
                            for dcw in range(8):
                                dc = dh * 8 + dcw
                                ins = v.scalar_tensor_tensor(
                                    out=dst[:, dcw * 512:(dcw + 1) * 512],
                                    in0=self.bank(dcw),
                                    scalar=self.sdv[:, dc:dc + 1],
                                    in1=xdn[:, dcw * 512:(dcw + 1) * 512],
                                    op0=OP.mult, op1=OP.add)
                            eV.inc(ins)
                        blk.vector(_v)

                        def _g(g, q=q, dh=dh, tt=tt, own=own):
                            g.wait_ge(eV.sem, eV.v)
                            tg = q * Q4 + tt * 512
                            if own:
                                dG.inc(g.dma_start(
                                    out=dram3(self.Y1OWN[dh * 1024:(dh + 1) * 1024,
                                                         tt * 512:(tt + 1) * 512]),
                                    in_=sb3(y1t[:, :], 8)))
                            else:
                                dG.inc(g.dma_start(
                                    out=dram3(self.Y1OTH[dh * 1024:(dh + 1) * 1024,
                                                         tg - 1024:tg - 512]),
                                    in_=sb3(y1tb[:, :], 8)))
                        blk.gpsimd(_g)

        if self.debug:
            def _s(s):
                s.wait_ge(self.dG.sem, self.dG.v)
                self.dS.inc(s.dma_start(out=self.dbg["dY1"][:, :], in_=self.Y1OWN[:, :]))
                self.dS.inc(s.dma_start(out=self.dbg["dY1O"][:, :], in_=self.Y1OTH[:, :]))
            self.block.sync(_s)

    @staticmethod
    def _dc(dh, dcw):
        return dh * 8 + dcw

    def p_qkv(self):
        nc, blk = self.nc, self.block
        dS, dW, dX, dG = self.dS, self.dW, self.dX, self.dG
        eT, eV, eA = self.eT, self.eV, self.eA

        for q in range(NQL):
            self.fence()
            with contextlib.ExitStack() as ph:
                sb = lambda n, s, d: ph.enter_context(nc.sbuf_tensor(self._un(n), s, d))
                xn = sb("xn2", [128, NDC * Q4], BF16)

                with contextlib.ExitStack() as lph:
                    ln = self.mk_ln(lph)
                    for tt2 in range(2):
                        tg = q * Q4 + tt2 * 1024
                        own = (q == 0 and tt2 == 0)

                        def _g(g, tg=tg, own=own, q=q, tt2=tt2):
                            g.wait_ge(eV.sem, eV.v)
                            if q == 0 and tt2 == 0:
                                g.wait_ge(dG.sem, dG.v)
                            if own:
                                self.dC.inc(g.dma_start(
                                    out=sb3(ln["xb"][:, :], NDC),
                                    in_=dram3(self.Y1OWN[:, :])))
                            else:
                                self.dC.inc(g.dma_start(
                                    out=sb3(ln["xb"][:, :], NDC),
                                    in_=dram3(self.Y1OTH[:, tg - 1024:tg])))
                        blk.gpsimd(_g)
                        self.emit_ln(xn, tt2 * 1024, Q4, ln)

                self.fence(light=True)
                wkv = [sb(f"wkv{i}", [128, NDC * 128], BF16) for i in range(2)]
                ktile = [sb(f"ktile{i}", [128, 512], BF16) for i in range(2)]
                vtt = [sb(f"vtt{i}", [128, 512], BF16) for i in range(2)]
                kt_dma = [self.dKo.v, self.dKo.v]
                kt_cons = [eT.v, eT.v]
                vt_dma = [self.dKo.v, self.dKo.v]
                tidx = 0

                jlist = list(range(16, 48)) + (list(range(0, 16)) if q == 0 else [])
                wk_pe_done = {}
                for idx, jc in enumerate(jlist):
                    b = idx % 2

                    def _s(s, jc=jc, b=b, idx=idx):
                        if idx >= 2:
                            s.wait_ge(eT.sem, wk_pe_done[idx - 2])
                        dW.inc(s.dma_start(
                            out=sb3(wkv[b][:, :], NDC),
                            in_=dram3(self.wqkv[:, jc * 128:(jc + 1) * 128])))
                    blk.gpsimd(_s)
                    w_loaded = dW.v

                    nt = 4 if jc >= 16 else 2
                    for tt in range(nt):
                        kb = 4 + tidx % 2
                        kpar = tidx % 2

                        def _t(t, b=b, tt=tt, kb=kb, w_loaded=w_loaded,
                               first=(tt == 0)):
                            if first:
                                t.wait_ge(dW.sem, w_loaded)
                            t.wait_ge(eV.sem, eV.v)
                            ins = None
                            for dc in range(NDC):
                                ins = t.matmul(self.bank(kb),
                                               wkv[b][:, dc * 128:(dc + 1) * 128],
                                               xn[:, dc * Q4 + tt * 512:
                                                  dc * Q4 + (tt + 1) * 512],
                                               start=(dc == 0), stop=(dc == NDC - 1))
                            eT.inc(ins)
                        blk.tensor(_t)
                        wk_pe_done[idx] = eT.v

                        def _v(v, jc=jc, kb=kb, kpar=kpar):
                            v.wait_ge(eT.sem, eT.v)
                            v.wait_ge(self.dKo.sem, kt_dma[kpar])
                            v.wait_ge(eT.sem, kt_cons[kpar])
                            eV.inc(v.tensor_scalar(out=ktile[kpar][:, :],
                                                   in0=self.bank(kb),
                                                   scalar1=self.bqv[:, jc:jc + 1],
                                                   scalar2=None, op0=OP.add))
                        blk.vector(_v)

                        tg = q * Q4 + tt * 512
                        if 16 <= jc < 32:
                            def _g(g, jc=jc, tg=tg, kpar=kpar):
                                g.wait_ge(eV.sem, eV.v)
                                self.dKo.inc(g.dma_start(
                                    out=self.KT[(jc - 16) * 128:(jc - 15) * 128,
                                                tg:tg + 512],
                                    in_=ktile[kpar][:, :]))
                                kt_dma[kpar] = self.dKo.v
                            blk.gpsimd(_g)
                        elif jc >= 32:
                            h = jc - 32
                            tb = 6 + tidx % 2

                            def _t(t, kpar=kpar, tb=tb):
                                t.wait_ge(eV.sem, eV.v)
                                ins = None
                                for i in range(4):
                                    ins = t.matmul(
                                        self.ps[:, tb * 512 + i * 128:
                                                tb * 512 + (i + 1) * 128],
                                        ktile[kpar][:, i * 128:(i + 1) * 128],
                                        self.ident[:, :], start=True, stop=True)
                                eT.inc(ins)
                            blk.tensor(_t)
                            kt_cons[kpar] = eT.v

                            def _v(v, kpar=kpar, tb=tb):
                                v.wait_ge(eT.sem, eT.v)
                                v.wait_ge(self.dKo.sem, vt_dma[kpar])
                                eV.inc(v.tensor_copy(out=vtt[kpar][:, :],
                                                     in_=self.ps[:, tb * 512:
                                                                 (tb + 1) * 512]))
                            blk.vector(_v)

                            def _g(g, h=h, tg=tg, kpar=kpar):
                                g.wait_ge(eV.sem, eV.v)
                                self.dKo.inc(g.dma_start(
                                    out=dram3(self.VT[h * T4 + tg:h * T4 + tg + 512, :]),
                                    in_=sb3(vtt[kpar][:, :], 4)))
                                vt_dma[kpar] = self.dKo.v
                            blk.gpsimd(_g)
                        else:
                            h = jc

                            def _v(v, h=h, tt=tt, kpar=kpar):
                                eV.inc(v.tensor_copy(
                                    out=self.QT[:, h * OWN + tt * 512:
                                                h * OWN + (tt + 1) * 512],
                                    in_=ktile[kpar][:, :]))
                            blk.vector(_v)
                        if jc < 16 or jc >= 32:
                            pass
                        tidx += 1

        self.kv_done_dKo = self.dKo.v

        if self.debug:
            def _s(s):
                s.wait_ge(dG.sem, dG.v)
                s.wait_ge(eV.sem, eV.v)
                self.dS.inc(s.dma_start(out=self.dbg["dKT"][:, :], in_=self.KT[:, :]))
                self.dS.inc(s.dma_start(out=self.dbg["dVT"][:, :], in_=self.VT[:, :]))
            blk.sync(_s)

            def _g(g):
                g.wait_ge(eV.sem, eV.v)
                dG.inc(g.dma_start(out=self.dbg["dQT"][:, :], in_=self.QT[:, :]))
            blk.gpsimd(_g)

    def p_exchange(self):
        dKo, eV = self.dKo, self.eV
        v0 = dKo.v
        norms_eV = eV.v

        def _g(g):
            g.wait_ge(dKo.sem, self.kv_ready_dKo)
            for arm in g.switch_core_id():
                p = arm.logical % 2
                g.dma_start(out=self.SHK[p][:, :], in_=self.KT[:, :]
                            ).then_inc(dKo.sem, 16)
                g.dma_start(
                    out=self.SHV[p].rearrange("(r c) b -> r (c b)", c=128),
                    in_=self.VT.rearrange("(r c) b -> r (c b)", c=128),
                ).then_inc(dKo.sem, 16)
                g.wait_ge(eV.sem, norms_eV)
                g.dma_start(out=self.SHKM[p][0:1, :], in_=self.k2m[0:1, :]
                            ).then_inc(dKo.sem, 16)
                g.wait_ge(dKo.sem, v0 + 48)
                g.dma_start(out=self.SHFLG[p][0:1, 0:1], in_=self.snon[0:1, 0:1]
                            ).then_inc(dKo.sem, 16)
                g.wait_ge(dKo.sem, v0 + 64)
                with g.register(f"xfl{int(arm)}") as fl, \
                     g.register(f"xnn{int(arm)}") as nn, \
                     g.register(f"xc{int(arm)}") as c, \
                     g.register(f"xit{int(arm)}") as it, \
                     g.register(f"xtm{int(arm)}") as tm:
                    g.load(nn, self.snon[0:1, 0:1])
                    g.reg_mov(c, 1)
                    g.reg_mov(it, 0)
                    with g.While(c):
                        g.load(fl, self.SHFLG[1 - p][0:1, 0:1])
                        g.reg_alu(c, fl, nn, OP.not_equal)
                        g.reg_add(it, it, 1)
                        g.reg_alu(tm, it, 20000000, OP.is_lt)
                        g.reg_alu(c, c, tm, OP.logical_and)
                    g.reg_save(self.polldbg[0:1, 0:1], it)
                    g.reg_save(self.polldbg[0:1, 1:2], fl)
                g.dma_start(out=self.pkm[0:1, :], in_=self.SHKM[1 - p][0:1, :]
                            ).then_inc(dKo.sem, 16)
        self.block.gpsimd(_g)
        dKo.v = v0 + 80

        def _v(v):
            v.wait_ge(dKo.sem, dKo.v)
            eV.inc(v.tensor_tensor(self.k2m[0:1, :], self.k2m[0:1, :],
                                   self.pkm[0:1, :], op=OP.max))
        self.block.vector(_v)

    def p_norms(self):
        nc, blk = self.nc, self.block
        eT, eV, eA = self.eT, self.eV, self.eA
        self.fence()
        with contextlib.ExitStack() as ph:
            sb = lambda n, s, d: ph.enter_context(nc.sbuf_tensor(self._un(n), s, d))
            kbuf = [sb(f"nk{i}", [128, T4], BF16) for i in range(2)]
            sq = [sb(f"nsq{i}", [128, T4], BF16) for i in range(2)]
            n11 = sb("n11", [1, 1], F32)
            staged = {}
            pe_done = {-1: eT.v, -2: eT.v}

            def _s(s):
                s.wait_ge(self.dKo.sem, self.dKo.v)
                for h in range(2):
                    self.dKV.inc(s.dma_start(out=kbuf[h][:, :],
                                             in_=self.KT[h * 128:(h + 1) * 128, :]))
                    staged[h] = self.dKV.v
            blk.sync(_s)

            for h in range(NH):
                bb = h % 2

                def _a(a, h=h, bb=bb):
                    a.wait_ge(self.dKV.sem, staged[h])
                    a.wait_ge(eT.sem, pe_done.get(h - 2, 0) if h >= 2 else 0)
                    eA.inc(a.activation(out=sq[bb][:, :], in_=kbuf[bb][:, :],
                                        func=AF.Square))
                blk.scalar(_a)

                def _t(t, h=h, bb=bb):
                    t.wait_ge(eA.sem, eA.v)
                    t.wait_ge(eV.sem, eV.v)
                    ins = None
                    for c in range(8):
                        ins = t.matmul(
                            self.ps[0:1, c * 512:(c + 1) * 512], self.ones[:, :],
                            sq[bb][:, c * 512:(c + 1) * 512],
                            start=True, stop=True)
                    eT.inc(ins)
                    pe_done[h] = eT.v
                blk.tensor(_t)

                def _v(v, h=h):
                    v.wait_ge(eT.sem, eT.v)
                    for c in range(8):
                        v.reduce_max(n11[0:1, 0:1], self.ps[0:1, c * 512:(c + 1) * 512],
                                     axis=AX.X)
                        v.tensor_tensor(self.k2m[0:1, h:h + 1], self.k2m[0:1, h:h + 1],
                                        n11[0:1, 0:1], op=OP.max)
                    eV.inc(v.tensor_copy(out=n11[0:1, 0:1], in_=n11[0:1, 0:1]))
                blk.vector(_v)

                if h + 2 < NH:
                    def _s(s, h=h, bb=bb):
                        s.wait_ge(eA.sem, eA.v)
                        self.dKV.inc(s.dma_start(
                            out=kbuf[bb][:, :],
                            in_=self.KT[(h + 2) * 128:(h + 3) * 128, :]))
                        staged[h + 2] = self.dKV.v
                    blk.sync(_s)

            for h in range(NH):
                def _a(a, h=h):
                    a.wait_ge(eV.sem, eV.v)
                    eA.inc(a.activation(out=sq[0][:, 0:OWN],
                                        in_=self.QT[:, h * OWN:(h + 1) * OWN],
                                        func=AF.Square))
                blk.scalar(_a)

                def _t(t, h=h):
                    t.wait_ge(eA.sem, eA.v)
                    t.wait_ge(eV.sem, eV.v)
                    ins = None
                    for half in range(2):
                        ins = t.matmul(self.ps[0:1, half * 512:(half + 1) * 512],
                                       self.ones[:, :],
                                       sq[0][:, half * 512:(half + 1) * 512],
                                       start=True, stop=True)
                    eT.inc(ins)
                blk.tensor(_t)

                def _v(v, h=h):
                    v.wait_ge(eT.sem, eT.v)
                    for half in range(2):
                        v.reduce_max(n11[0:1, 0:1],
                                     self.ps[0:1, half * 512:(half + 1) * 512],
                                     axis=AX.X)
                        v.tensor_tensor(self.q2m[0:1, h:h + 1], self.q2m[0:1, h:h + 1],
                                        n11[0:1, 0:1], op=OP.max)
                    eV.inc(v.tensor_copy(out=n11[0:1, 0:1], in_=n11[0:1, 0:1]))
                blk.vector(_v)

    def p_bias(self):
        nc, blk = self.nc, self.block
        eT, eV, eA = self.eT, self.eV, self.eA

        self.fence()
        with contextlib.ExitStack() as ph:
            sb = lambda n, s, d: ph.enter_context(nc.sbuf_tensor(self._un(n), s, d))
            t16 = sb("t16", [1, NH], F32)
            t16b = sb("t16b", [1, NH], F32)
            rowf = sb("rowf", [1, NKC + 1], F32)
            rowb = sb("rowb", [1, NKC + 1], BF16)

            def _a(a):
                a.wait_ge(eV.sem, eV.v)
                a.activation(out=t16[0:1, :], in_=self.k2m[0:1, :], func=AF.Sqrt)
                eA.inc(a.activation(out=t16b[0:1, :], in_=self.q2m[0:1, :],
                                    func=AF.Sqrt))
            blk.scalar(_a)

            def _v(v):
                v.wait_ge(eA.sem, eA.v)
                v.tensor_tensor(t16[0:1, :], t16[0:1, :], t16b[0:1, :], op=OP.mult)
                eV.inc(v.tensor_scalar_mul(self.nbh[0:1, :], t16[0:1, :], -ISCALE))
            blk.vector(_v)

            for h in range(NH):
                def _v(v, h=h):
                    v.wait_ge(eV.sem, eV.v)
                    v.wait_ge(eT.sem, eT.v)
                    v.tensor_scalar(out=rowf[0:1, 0:NKC], in0=self.mrow[0:1, :],
                                    scalar1=self.nbh[0:1, h:h + 1], scalar2=None,
                                    op0=OP.add)
                    v.tensor_copy(out=rowf[0:1, NKC:NKC + 1],
                                  in_=self.nbh[0:1, h:h + 1])
                    eV.inc(v.tensor_copy(out=rowb[0:1, :], in_=rowf[0:1, :]))
                blk.vector(_v)

                def _t(t, h=h):
                    t.wait_ge(eV.sem, eV.v)
                    eT.inc(t.matmul(self.ps[:, 1024:1024 + NKC + 1],
                                    self.o1x128[:, :], rowb[0:1, :],
                                    start=True, stop=True))
                blk.tensor(_t)

                def _v(v, h=h):
                    v.wait_ge(eT.sem, eT.v)
                    eV.inc(v.tensor_copy(
                        out=self.biasT[:, h * (NKC + 1):(h + 1) * (NKC + 1)],
                        in_=self.ps[:, 1024:1024 + NKC + 1]))
                blk.vector(_v)

        if self.debug:
            def _g(g):
                g.wait_ge(eV.sem, eV.v)
                self.dG.inc(g.dma_start(out=self.dbg["dNB"][:, :], in_=self.nbh[0:1, :]))
                self.dG.inc(g.dma_start(out=self.dbg["dNB2"][0:1, :], in_=self.k2m[0:1, :]))
                self.dG.inc(g.dma_start(out=self.dbg["dNB2"][1:2, :], in_=self.q2m[0:1, :]))
                self.dG.inc(g.dma_start(out=self.dbg["dNB2"][2:3, :], in_=self.pkm[0:1, :]))
                for s_ in range(2):
                    self.dG.inc(g.dma_start(out=self.dbg["dSHKM"][s_:s_ + 1, :],
                                            in_=self.SHKM[s_][0:1, :]))
                    self.dG.inc(g.dma_start(out=self.dbg["dSHFL"][s_:s_ + 1, :],
                                            in_=self.SHFLG[s_][0:1, :]))
                self.dG.inc(g.dma_start(out=self.dbg["dPOLL"][:, :],
                                        in_=self.polldbg[0:1, :]))
            blk.gpsimd(_g)

    def p_attention(self):
        nc, blk = self.nc, self.block
        dKV, dG = self.dKV, self.dG
        eT, eV, eA = self.eT, self.eV, self.eA
        NB = NKC + 1

        self.fence()
        with contextlib.ExitStack() as ph:
            sb = lambda n, s, d: ph.enter_context(nc.sbuf_tensor(self._un(n), s, d))
            kth = [sb(f"kth{i}", [128, T], BF16) for i in range(2)]
            vh = [sb(f"vh{i}", [128, T], BF16) for i in range(2)]
            kdg = [sb(f"kdg{i}", [128, OWN], BF16) for i in range(2)]
            vdg = [sb(f"vdg{i}", [128, OWN], BF16) for i in range(2)]
            pt = [sb(f"pt{i}", [128, OWN], BF16) for i in range(4)]
            ptsum = [sb(f"ptsum{i}", [128, OWN], BF16) for i in range(2)]
            dmk = sb("dmk", [128, 8 * OWN], F32)
            rs = sb("rs", [1, OWN], F32)
            rsb = sb("rsb", [1, OWN], BF16)
            rse = sb("rse", [1, OWN], F32)
            rseb = sb("rseb", [1, OWN], BF16)
            rcs = sb("rcs", [128, OWN], F32)

            def _s(s):
                self.dS.inc(s.dma_start(
                    out=sb3(dmk[:, :], 8),
                    in_=dram3(self.diagmask[:, :])))
            blk.sync(_s)

            head_pe_done = {-1: eT.v, -2: eT.v}
            staged = {}

            def stage(s, h):
                b = h % 2
                if h >= 2:
                    s.wait_ge(eT.sem, head_pe_done[h - 2])
                if h == 0:
                    s.wait_ge(self.dKo.sem, self.dKo.v)
                for sl in range(2):
                    dKV.inc(s.dma_start(
                        out=kth[b][:, sl * T4:(sl + 1) * T4],
                        in_=self.SHK[sl][h * 128:(h + 1) * 128, :]))
                    for j in range(2):
                        r0 = h * T4 + j * Q4
                        dKV.inc(s.dma_start(
                            out=sb3(vh[b][:, sl * T4 + j * Q4:sl * T4 + (j + 1) * Q4], 16),
                            in_=dram3(self.SHV[sl][r0:r0 + Q4, :])))
                dKV.inc(s.dma_start(out=kdg[b][:, :],
                                    in_=self.KT[h * 128:(h + 1) * 128, 0:OWN]))
                dKV.inc(s.dma_start(out=sb3(vdg[b][:, :], 8),
                                    in_=dram3(self.VT[h * T4:h * T4 + OWN, :])))
                staged[h] = dKV.v

            def _s(s):
                stage(s, 0)
            blk.sync(_s)

            for h in range(NH):
                b = h % 2
                eA0 = eA.v
                eV0 = eV.v
                NIT = NKC + 8

                def srcs(i, b):
                    if i < NKC:
                        return (kth[b][:, i * 128:(i + 1) * 128],
                                vh[b][:, i * 128:(i + 1) * 128], None, i)
                    d = i - NKC
                    return (kdg[b][:, d * 128:(d + 1) * 128],
                            vdg[b][:, d * 128:(d + 1) * 128], d, NKC)

                pe_base = eT.v
                NPAIR = NIT // 2
                s_done = {}
                sum_done = {}
                c = pe_base + 1
                s_done[0] = c
                for i in range(NIT):
                    if i + 1 < NIT:
                        c += 1
                        s_done[i + 1] = c
                    c += 1
                    if i % 2 == 1:
                        c += 1
                        sum_done[i // 2] = c
                pe_end = c
                mask_done = {}
                pa_done = {}
                cv = eV0
                for i in range(NIT):
                    if i >= NKC:
                        cv += 1
                        mask_done[i] = cv
                    if i % 2 == 1:
                        cv += 1
                        pa_done[i // 2] = cv
                vec_end = cv

                def _t(t, h=h, b=b, eA0=eA0, eV0=eV0, pe_end=pe_end,
                       pa_done=pa_done):
                    t.wait_ge(dKV.sem, staged[h])
                    t.wait_ge(eV.sem, eV0)

                    def emit_s(i):
                        ksrc, _, _, _ = srcs(i, b)
                        sb_ = (i % 2) * 2
                        t.matmul(self.ps[:, sb_ * 512:sb_ * 512 + 512],
                                 ksrc, self.QT[:, h * OWN:h * OWN + 512],
                                 start=True, stop=True)
                        ins = t.matmul(self.ps[:, (sb_ + 1) * 512:(sb_ + 1) * 512 + 512],
                                       ksrc, self.QT[:, h * OWN + 512:(h + 1) * OWN],
                                       start=True, stop=True)
                        eT.inc(ins)

                    emit_s(0)
                    for i in range(NIT):
                        if i + 1 < NIT:
                            if i >= 1:
                                t.wait_ge(eA.sem, eA0 + i)
                            emit_s(i + 1)
                        _, vsrc, _, _ = srcs(i, b)
                        t.wait_ge(eA.sem, eA0 + i + 1)
                        st = (i == 0)
                        sp = (i == NIT - 1)
                        t.matmul(self.ps[:, 4 * 512:4 * 512 + 512],
                                 vsrc, pt[i % 4][:, 0:512], start=st, stop=sp)
                        ins = t.matmul(self.ps[:, 5 * 512:5 * 512 + 512],
                                       vsrc, pt[i % 4][:, 512:1024],
                                       start=st, stop=sp)
                        eT.inc(ins)
                        if i % 2 == 1:
                            p = i // 2
                            t.wait_ge(eV.sem, pa_done[p])
                            stp = (p == 0)
                            spp = (p == NPAIR - 1)
                            t.matmul(self.ps[0:1, 6 * 512:6 * 512 + 512],
                                     self.ones[:, :], ptsum[p % 2][:, 0:512],
                                     start=stp, stop=spp)
                            ins = t.matmul(self.ps[0:1, 7 * 512:7 * 512 + 512],
                                           self.ones[:, :], ptsum[p % 2][:, 512:1024],
                                           start=stp, stop=spp)
                            eT.inc(ins)
                    head_pe_done[h] = eT.v
                    assert eT.v == pe_end
                blk.tensor(_t)

                def _v(v, h=h, s_done=s_done, sum_done=sum_done, eA0=eA0,
                       vec_end=vec_end):
                    for i in range(NIT):
                        if i >= NKC:
                            d = i - NKC
                            sb_ = (i % 2) * 2
                            v.wait_ge(eT.sem, s_done[i])
                            eV.inc(v.tensor_tensor(
                                self.ps[:, sb_ * 512:(sb_ + 2) * 512],
                                self.ps[:, sb_ * 512:(sb_ + 2) * 512],
                                dmk[:, d * OWN:(d + 1) * OWN], op=OP.add))
                        if i % 2 == 1:
                            p = i // 2
                            v.wait_ge(eA.sem, eA0 + i + 1)
                            if p >= 2:
                                v.wait_ge(eT.sem, sum_done[p - 2])
                            eV.inc(v.tensor_tensor(
                                ptsum[p % 2][:, :], pt[(i - 1) % 4][:, :],
                                pt[i % 4][:, :], op=OP.add))
                    assert eV.v == vec_end
                blk.vector(_v)

                def _a(a, h=h, s_done=s_done, mask_done=mask_done,
                       pa_done=pa_done):
                    for i in range(NIT):
                        sb_ = (i % 2) * 2
                        if i >= NKC:
                            a.wait_ge(eV.sem, mask_done[i])
                            col = h * NB + NKC
                        else:
                            a.wait_ge(eT.sem, s_done[i])
                            if i >= 4:
                                a.wait_ge(eV.sem, pa_done[(i - 4) // 2])
                            col = h * NB + i
                        eA.inc(a.activation(out=pt[i % 4][:, :],
                                            in_=self.ps[:, sb_ * 512:(sb_ + 2) * 512],
                                            func=AF.Exp,
                                            bias=self.biasT[:, col:col + 1],
                                            scale=ISCALE))
                blk.scalar(_a)

                def _v(v, h=h):
                    v.wait_ge(eT.sem, head_pe_done[h])
                    v.reciprocal(rs[0:1, :], self.ps[0:1, 6 * 512:8 * 512])
                    v.tensor_copy(out=rsb[0:1, :], in_=rs[0:1, :])
                    v.tensor_copy(out=rse[0:1, :], in_=rsb[0:1, :])
                    v.tensor_tensor(rse[0:1, :], rs[0:1, :], rse[0:1, :],
                                    op=OP.subtract)
                    eV.inc(v.tensor_copy(out=rseb[0:1, :], in_=rse[0:1, :]))
                blk.vector(_v)

                def _t(t, h=h):
                    t.wait_ge(eV.sem, eV.v)
                    t.matmul(self.ps[:, 0:512], self.o1x128[:, :], rsb[0:1, 0:512],
                             start=True, stop=False)
                    t.matmul(self.ps[:, 0:512], self.o1x128[:, :], rseb[0:1, 0:512],
                             start=False, stop=True)
                    t.matmul(self.ps[:, 512:1024], self.o1x128[:, :],
                             rsb[0:1, 512:1024], start=True, stop=False)
                    eT.inc(t.matmul(self.ps[:, 512:1024], self.o1x128[:, :],
                                    rseb[0:1, 512:1024], start=False, stop=True))
                blk.tensor(_t)

                def _v(v, h=h):
                    v.wait_ge(eT.sem, eT.v)
                    v.tensor_copy(out=rcs[:, :], in_=self.ps[:, 0:1024])
                    eV.inc(v.tensor_tensor(self.AT[:, h * OWN:(h + 1) * OWN],
                                           self.ps[:, 4 * 512:6 * 512],
                                           rcs[:, :], op=OP.mult))
                blk.vector(_v)

                if h + 1 < NH:
                    def _s(s, h=h):
                        stage(s, h + 1)
                    blk.sync(_s)

        if self.debug:
            def _g(g):
                g.wait_ge(eV.sem, eV.v)
                dG.inc(g.dma_start(out=self.dbg["dAT"][:, :], in_=self.AT[:, :]))
            blk.gpsimd(_g)

    def p_outproj(self):
        nc, blk = self.nc, self.block
        dW, dX, dG = self.dW, self.dX, self.dG
        eT, eV = self.eT, self.eV

        self.fence()
        with contextlib.ExitStack() as ph:
            sb = lambda n, s, d: ph.enter_context(nc.sbuf_tensor(self._un(n), s, d))
            wo = [sb(f"wo{i}", [128, NDC * 128], BF16) for i in range(2)]
            y1o = sb("y1o", [128, OWN], F32)
            y2t = sb("y2t", [128, OWN], F32)
            oc_pe_done = {}

            for oc in range(NDC):
                b = oc % 2

                def _g(g, oc=oc, b=b):
                    if oc >= 2:
                        g.wait_ge(eT.sem, oc_pe_done[oc - 2])
                    dW.inc(g.dma_start(
                        out=sb3(wo[b][:, :], NDC),
                        in_=dram3(self.wout[:, oc * 128:(oc + 1) * 128])))
                blk.gpsimd(_g)

                def _s(s, oc=oc):
                    s.wait_ge(eV.sem, eV.v)
                    dX.inc(s.dma_start(
                        out=y1o[:, :],
                        in_=self.Y1OWN[oc * 128:(oc + 1) * 128, :]))
                blk.sync(_s)
                w_loaded = dW.v
                x_loaded = dX.v

                par = oc % 2
                pb = 0 if par == 0 else 2

                def _t(t, b=b, pb=pb, w_loaded=w_loaded):
                    t.wait_ge(dW.sem, w_loaded)
                    t.wait_ge(eV.sem, eV.v)
                    for dc in range(NDC):
                        t.matmul(self.ps[:, pb * 512:pb * 512 + 512],
                                 wo[b][:, dc * 128:(dc + 1) * 128],
                                 self.AT[:, dc * OWN:dc * OWN + 512],
                                 start=(dc == 0), stop=(dc == NDC - 1))
                    ins = None
                    for dc in range(NDC):
                        ins = t.matmul(self.ps[:, (pb + 1) * 512:(pb + 1) * 512 + 512],
                                       wo[b][:, dc * 128:(dc + 1) * 128],
                                       self.AT[:, dc * OWN + 512:(dc + 1) * OWN],
                                       start=(dc == 0), stop=(dc == NDC - 1))
                    eT.inc(ins)
                blk.tensor(_t)
                oc_pe_done[oc] = eT.v

                def _v(v, oc=oc, pb=pb, x_loaded=x_loaded):
                    v.wait_ge(eT.sem, eT.v)
                    v.wait_ge(dX.sem, x_loaded)
                    v.wait_ge(dG.sem, dG.v)
                    eV.inc(v.scalar_tensor_tensor(
                        out=y2t[:, :], in0=self.ps[:, pb * 512:(pb + 2) * 512],
                        scalar=self.bov[:, oc:oc + 1], in1=y1o[:, :],
                        op0=OP.add, op1=OP.add))
                blk.vector(_v)

                def _g(g, oc=oc):
                    g.wait_ge(eV.sem, eV.v)
                    dG.inc(g.dma_start(out=self.Y2[oc * 128:(oc + 1) * 128, :],
                                       in_=y2t[:, :]))
                blk.gpsimd(_g)

        if self.debug:
            def _s(s):
                s.wait_ge(dG.sem, dG.v)
                self.dS.inc(s.dma_start(out=self.dbg["dY2"][:, :], in_=self.Y2[:, :]))
            blk.sync(_s)

    def p_ffn3(self):
        nc, blk = self.nc, self.block
        dW, dH, dX, dG = self.dW, self.dH, self.dX, self.dG
        eT, eV, eA = self.eT, self.eV, self.eA

        self.fence()
        with contextlib.ExitStack() as ph:
            sb = lambda n, s, d: ph.enter_context(nc.sbuf_tensor(self._un(n), s, d))
            h3 = sb("h3", [128, NHC * OWN], BF16)

            with contextlib.ExitStack() as gph:
                gsb = lambda n, s, d: gph.enter_context(nc.sbuf_tensor(self._un(n), s, d))
                xn = gsb("xn3", [128, NDC * OWN], BF16)

                with contextlib.ExitStack() as lph:
                    ln = self.mk_ln(lph, tw=512)
                    for tt2 in range(2):
                        def _g(g, tt2=tt2):
                            g.wait_ge(eV.sem, eV.v)
                            if tt2 == 0:
                                g.wait_ge(dG.sem, dG.v)
                            self.dC.inc(g.dma_start(
                                out=sb3(ln["xb"][:, :], NDC),
                                in_=dram3(self.Y2[:, tt2 * 512:(tt2 + 1) * 512])))
                        blk.gpsimd(_g)
                        self.emit_ln(xn, tt2 * 512, OWN, ln, tw=512)

                self.fence(light=True)
                wgT = [gsb(f"wgT3{i}", [128, NDC * 512], BF16) for i in range(2)]
                wuT = [gsb(f"wuT3{i}", [128, NDC * 512], BF16) for i in range(2)]
                gt = gsb("gt3", [128, 512], BF16)
                slab_pe_done = {}
                prev_dve = [eV.v, eV.v]
                prev_act = [eA.v, eA.v]

                for hs in range(NHC // 4):
                    b = hs % 2

                    def _s(s, hs=hs, b=b):
                        if hs >= 2:
                            s.wait_ge(eT.sem, slab_pe_done[hs - 2])
                        for dc in range(NDC):
                            dW.inc(s.dma_start_transpose(
                                out=wgT[b][:, dc * 512:(dc + 1) * 512],
                                in_=self.WgB[hs * 512:(hs + 1) * 512,
                                             dc * 128:(dc + 1) * 128]))
                        for dc in range(NDC):
                            dW.inc(s.dma_start_transpose(
                                out=wuT[b][:, dc * 512:(dc + 1) * 512],
                                in_=self.WuB[hs * 512:(hs + 1) * 512,
                                             dc * 128:(dc + 1) * 128]))
                    blk.sync(_s)
                    slab_loaded = dW.v

                    for hcw in range(4):
                        hc = hs * 4 + hcw
                        for tt in range(2):
                            par = (hcw * 2 + tt) % 2
                            gb, ub = (4, 5) if par == 0 else (6, 7)

                            def _t(t, b=b, hcw=hcw, tt=tt, gb=gb, ub=ub, par=par,
                                   first=(hcw == 0 and tt == 0),
                                   slab_loaded=slab_loaded, hs=hs):
                                if first:
                                    t.wait_ge(dW.sem, slab_loaded)
                                    if hs == 0:
                                        t.wait_ge(eV.sem, eV.v)
                                t.wait_ge(eV.sem, prev_dve[par])
                                t.wait_ge(eA.sem, prev_act[par])
                                for dc in range(NDC):
                                    t.matmul(self.bank(gb),
                                             wgT[b][:, dc * 512 + hcw * 128:
                                                    dc * 512 + hcw * 128 + 128],
                                             xn[:, dc * OWN + tt * 512:
                                                dc * OWN + (tt + 1) * 512],
                                             start=(dc == 0), stop=(dc == NDC - 1))
                                ins = None
                                for dc in range(NDC):
                                    ins = t.matmul(self.bank(ub),
                                                   wuT[b][:, dc * 512 + hcw * 128:
                                                          dc * 512 + hcw * 128 + 128],
                                                   xn[:, dc * OWN + tt * 512:
                                                      dc * OWN + (tt + 1) * 512],
                                                   start=(dc == 0), stop=(dc == NDC - 1))
                                eT.inc(ins)
                            blk.tensor(_t)
                            slab_pe_done[hs] = eT.v

                            def _a(a, hc=hc, gb=gb):
                                a.wait_ge(eT.sem, eT.v)
                                a.wait_ge(eV.sem, eV.v)
                                eA.inc(a.activation(out=gt[:, :], in_=self.bank(gb),
                                                    func=AF.Silu,
                                                    scale=self.sgv[:, hc:hc + 1]))
                            blk.scalar(_a)
                            prev_act[par] = eA.v

                            def _v(v, hc=hc, tt=tt, ub=ub):
                                v.wait_ge(eA.sem, eA.v)
                                eV.inc(v.scalar_tensor_tensor(
                                    out=h3[:, hc * OWN + tt * 512:
                                           hc * OWN + (tt + 1) * 512],
                                    in0=self.bank(ub),
                                    scalar=self.suv[:, hc:hc + 1], in1=gt[:, :],
                                    op0=OP.mult, op1=OP.mult))
                            blk.vector(_v)
                            prev_dve[par] = eV.v

            self.fence()
            with contextlib.ExitStack() as dph:
                dsb = lambda n, s, d: dph.enter_context(nc.sbuf_tensor(self._un(n), s, d))
                wdT = dsb("wdT3", [128, NHC * 512], BF16)
                y2dn = dsb("y2dn", [128, 4 * 512], F32)
                y3t = dsb("y3t", [128, 4 * 512], F32)

                prev_down_evac = [eV.v, eV.v]
                for dq in range(4):
                    wd_tick = {}

                    def _s(s, dq=dq):
                        s.wait_ge(eT.sem, eT.v)
                        for hc in range(NHC):
                            dW.inc(s.dma_start_transpose(
                                out=wdT[:, hc * 512:(hc + 1) * 512],
                                in_=self.WdB[dq * 512:(dq + 1) * 512,
                                             hc * 128:(hc + 1) * 128]))
                            wd_tick[hc] = dW.v
                    blk.sync(_s)

                    for tt in range(2):
                        pb = (tt % 2) * 4

                        def _s(s, dq=dq, tt=tt):
                            s.wait_ge(eV.sem, eV.v)
                            dX.inc(s.dma_start(
                                out=sb3(y2dn[:, :], 4),
                                in_=dram3(self.Y2[dq * 512:(dq + 1) * 512,
                                                  tt * 512:(tt + 1) * 512])))
                        blk.sync(_s)
                        y2_loaded = dX.v

                        def _t(t, dq=dq, tt=tt, pb=pb):
                            if tt == 0 and dq == 0:
                                t.wait_ge(eV.sem, eV.v)
                            t.wait_ge(eV.sem, prev_down_evac[tt % 2])
                            ins = None
                            for hc in range(NHC):
                                if tt == 0:
                                    t.wait_ge(dW.sem, wd_tick[hc])
                                for dcw in range(4):
                                    ins = t.matmul(
                                        self.bank(pb + dcw),
                                        wdT[:, hc * 512 + dcw * 128:
                                            hc * 512 + dcw * 128 + 128],
                                        h3[:, hc * OWN + tt * 512:
                                           hc * OWN + (tt + 1) * 512],
                                        start=(hc == 0), stop=(hc == NHC - 1))
                            eT.inc(ins)
                        blk.tensor(_t)

                        def _v(v, dq=dq, tt=tt, pb=pb, y2_loaded=y2_loaded):
                            v.wait_ge(eT.sem, eT.v)
                            v.wait_ge(dX.sem, y2_loaded)
                            v.wait_ge(dG.sem, dG.v)
                            ins = None
                            for dcw in range(4):
                                dc = dq * 4 + dcw
                                ins = v.scalar_tensor_tensor(
                                    out=y3t[:, dcw * 512:(dcw + 1) * 512],
                                    in0=self.bank(pb + dcw),
                                    scalar=self.sdv[:, dc:dc + 1],
                                    in1=y2dn[:, dcw * 512:(dcw + 1) * 512],
                                    op0=OP.mult, op1=OP.add)
                            eV.inc(ins)
                            prev_down_evac[tt % 2] = eV.v
                        blk.vector(_v)

                        def _g(g, dq=dq, tt=tt):
                            g.wait_ge(eV.sem, eV.v)
                            dG.inc(g.dma_start(
                                out=dram3(self.y3T[dq * 512:(dq + 1) * 512,
                                                   tt * 512:(tt + 1) * 512]),
                                in_=sb3(y3t[:, :], 4)))
                        blk.gpsimd(_g)

    def p_final(self):
        def _g(g):
            g.wait_ge(self.dG.sem, self.dG.v)
        self.block.gpsimd(_g)

        def _s(s):
            s.wait_ge(self.dS.sem, self.dS.v)
        self.block.sync(_s)


_CACHE = {}


def get_nc(debug=False):
    key = bool(debug)
    if key not in _CACHE:
        b = Builder(debug=debug)
        _CACHE[key] = (b.build(), b)
    return _CACHE[key]


def make_inputs(inputs, debug=False):
    x = np.asarray(inputs["L0_attn_x"], dtype=np.float32)[0]
    w_qkv = np.asarray(inputs["L0_attn_w_qkv"], dtype=np.float32)
    b_qkv = np.asarray(inputs["L0_attn_b_qkv"], dtype=np.float32).reshape(3 * D, 1)
    w_out = np.asarray(inputs["L0_attn_w_out"], dtype=np.float32)
    b_out = np.asarray(inputs["L0_attn_b_out"], dtype=np.float32).reshape(D, 1)
    wq_g = np.ascontiguousarray(np.asarray(inputs["L0_ffn_wq_gate"], dtype=np.int32))
    s_g = np.asarray(inputs["L0_ffn_s_gate"], dtype=np.float32).reshape(HID, 1)
    wq_u = np.ascontiguousarray(np.asarray(inputs["L0_ffn_wq_up"], dtype=np.int32))
    s_u = np.asarray(inputs["L0_ffn_s_up"], dtype=np.float32).reshape(HID, 1)
    wq_d = np.ascontiguousarray(np.asarray(inputs["L0_ffn_wq_down"], dtype=np.int32))
    s_d = np.asarray(inputs["L0_ffn_s_down"], dtype=np.float32).reshape(D, 1)

    ident = np.eye(128, dtype=BF)
    ones = np.ones((128, 1), dtype=BF)
    o1x128 = np.ones((1, 128), dtype=BF)
    qi = np.arange(OWN)
    dmask = np.where(qi[None, :] >= np.arange(OWN)[:, None], 0.0, NEG
                     ).astype(np.float32)

    common = dict(wqg=wq_g, wqu=wq_u, wqd=wq_d, sgx=s_g, sux=s_u, sdx=s_d,
                  wqkv=np.ascontiguousarray(w_qkv), bqkv=b_qkv,
                  wout=np.ascontiguousarray(w_out), boutx=b_out,
                  diagmask=dmask, identin=ident, onesin=ones, onesk1=o1x128)

    import time
    nonce = np.array([[int(time.time_ns()) & 0x7FFFFFFF]], dtype=np.int32)
    common["noncex"] = nonce

    def member_blocks(c):
        return [c] + [b for b in range(8) if b % 2 == c % 2 and b != c]

    in_maps = []
    perms = []
    for c in range(NCORE):
        blocks = member_blocks(c)
        perms.append(blocks)
        tok = np.concatenate([np.arange(b * OWN, (b + 1) * OWN) for b in blocks])
        xTc = np.ascontiguousarray(x[tok, :].T)
        mr = np.full((1, NKC), NEG, dtype=np.float32)
        for kc in range(NKC):
            sl = kc // 32
            member = (c // 2) * 2 + sl
            blk_g = member_blocks(member)[(kc % 32) // 8]
            if blk_g < c:
                mr[0, kc] = 0.0
        m = dict(common)
        m["xT"] = xTc
        m["maskrow"] = mr
        in_maps.append(m)
    return in_maps, perms


def kernel(**inputs):
    nc, _ = get_nc(debug=False)
    in_maps, _ = make_inputs(inputs)
    res = run_bass_kernel_spmd(nc, in_maps, core_ids=list(range(NCORE)))
    out = np.empty((1, T, D), dtype=np.float32)
    for c in range(NCORE):
        out[0, c * OWN:(c + 1) * OWN, :] = res.results[c]["y3T"].T
    return out


# version 09
# speedup vs baseline: 2.1920x; 2.1920x over previous
import contextlib
import math

import numpy as np
import ml_dtypes

import concourse.bass as bass
import concourse.mybir as mybir
from concourse.bass_utils import run_bass_kernel_spmd

F32 = mybir.dt.float32
BF16 = mybir.dt.bfloat16
I32 = mybir.dt.int32
AF = mybir.ActivationFunctionType
OP = mybir.AluOpType
AX = mybir.AxisListType

D = 2048
HID = 5632
NH = 16
DH = 128
NDC = D // 128
NHC = HID // 128
ISCALE = 1.0 / math.sqrt(DH)
NEG = -1.0e9

T = 8192
T4 = 4096
OWN = 1024
Q4 = 2048
NQL = T4 // Q4
NKC = T // 128
NCORE = 8

BF = ml_dtypes.bfloat16


class Ctr:
    def __init__(self, sem, step=1):
        self.sem = sem
        self.step = step
        self.v = 0

    def inc(self, instr, n=None):
        n = self.step if n is None else n
        instr.then_inc(self.sem, n)
        self.v += n
        return self.v



def dram3(ap2d, p=128):
    return ap2d.rearrange("(c p) t -> p c t", p=p)


def sb3(ap2d, c):
    return ap2d.rearrange("p (c t) -> p c t", c=c)


class Builder:
    _uid = 0

    @classmethod
    def _un(cls, n):
        cls._uid += 1
        return f"{n}_{cls._uid}"

    def __init__(self, debug=False):
        self.debug = debug
        nc = bass.Bass("TRN2", target_bir_lowering=False, debug=False)
        self.nc = nc
        ap = lambda *a, **k: nc.dram_tensor(*a, **k).ap()
        self.xT = ap("xT", [D, T4], F32, kind="ExternalInput")
        self.noncex = ap("noncex", [1, 1], I32, kind="ExternalInput")
        self.wqg = ap("wqg", [HID, D], I32, kind="ExternalInput")
        self.wqu = ap("wqu", [HID, D], I32, kind="ExternalInput")
        self.wqd = ap("wqd", [D, HID], I32, kind="ExternalInput")
        self.sgx = ap("sgx", [HID, 1], F32, kind="ExternalInput")
        self.sux = ap("sux", [HID, 1], F32, kind="ExternalInput")
        self.sdx = ap("sdx", [D, 1], F32, kind="ExternalInput")
        self.wqkv = ap("wqkv", [D, 3 * D], F32, kind="ExternalInput")
        self.bqkv = ap("bqkv", [3 * D, 1], F32, kind="ExternalInput")
        self.wout = ap("wout", [D, D], F32, kind="ExternalInput")
        self.boutx = ap("boutx", [D, 1], F32, kind="ExternalInput")
        self.maskrow = ap("maskrow", [1, NKC], F32, kind="ExternalInput")
        self.diagmask = ap("diagmask", [8 * 128, OWN], F32, kind="ExternalInput")
        self.identin = ap("identin", [128, 128], BF16, kind="ExternalInput")
        self.onesin = ap("onesin", [128, 1], BF16, kind="ExternalInput")
        self.onesk1 = ap("onesk1", [1, 128], BF16, kind="ExternalInput")

        self.y3T = ap("y3T", [D, OWN], F32, kind="ExternalOutput")

        self.WgB = ap("WgB", [HID, D], BF16)
        self.WuB = ap("WuB", [HID, D], BF16)
        self.WdB = ap("WdB", [D, HID], BF16)
        self.WqkvB = ap("WqkvB", [D, 3 * D], BF16)
        self.WoutB = ap("WoutB", [D, D], BF16)
        self.HT1 = ap("HT1", [HID, Q4], BF16)
        self.KT = ap("KT", [D, T4], BF16)
        self.VT = ap("VT", [NH * T4, DH], BF16)
        self.Y1OWN = ap("Y1OWN", [D, OWN], F32)
        self.Y1OTH = ap("Y1OTH", [D, T4 - OWN], BF16)
        self.Y2 = ap("Y2", [D, OWN], F32)
        self.SHK = [ap(f"SHK{s}", [D, T4], BF16, addr_space="Shared") for s in range(2)]
        self.SHV = [ap(f"SHV{s}", [NH * T4, DH], BF16, addr_space="Shared") for s in range(2)]
        self.SHKM = [ap(f"SHKM{s}", [1, NH], F32, addr_space="Shared") for s in range(2)]
        self.SHFLG = [ap(f"SHFLG{s}", [1, 2], I32, addr_space="Shared") for s in range(2)]

        self.dbg = {}
        if debug:
            self.dbg["dKT"] = ap("dKT", [D, T4], BF16, kind="ExternalOutput")
            self.dbg["dVT"] = ap("dVT", [NH * T4, DH], BF16, kind="ExternalOutput")
            self.dbg["dY1"] = ap("dY1", [D, OWN], F32, kind="ExternalOutput")
            self.dbg["dY1O"] = ap("dY1O", [D, T4 - OWN], BF16, kind="ExternalOutput")
            self.dbg["dQT"] = ap("dQT", [128, NH * OWN], BF16, kind="ExternalOutput")
            self.dbg["dAT"] = ap("dAT", [128, NDC * OWN], BF16, kind="ExternalOutput")
            self.dbg["dY2"] = ap("dY2", [D, OWN], F32, kind="ExternalOutput")
            self.dbg["dNB"] = ap("dNB", [1, NH], F32, kind="ExternalOutput")
            self.dbg["dNB2"] = ap("dNB2", [3, NH], F32, kind="ExternalOutput")
            self.dbg["dSHKM"] = ap("dSHKM", [2, NH], F32, kind="ExternalOutput")
            self.dbg["dSHFL"] = ap("dSHFL", [2, 2], I32, kind="ExternalOutput")
            self.dbg["dPOLL"] = ap("dPOLL", [1, 2], I32, kind="ExternalOutput")
            self.dbg["dHT"] = ap("dHT", [HID, Q4], BF16, kind="ExternalOutput")
            self.dbg["dXN1"] = ap("dXN1", [D, T4], BF16, kind="ExternalOutput")

    def build(self):
        nc = self.nc
        with contextlib.ExitStack() as top:
            sb = lambda n, s, d: top.enter_context(nc.sbuf_tensor(self._un(n), s, d))
            self.ident = sb("ident", [128, 128], BF16)
            self.ones = sb("ones", [128, 1], BF16)
            self.o1x128 = sb("o1x128", [1, 128], BF16)
            self.sgv = sb("sgv", [128, NHC], F32)
            self.suv = sb("suv", [128, NHC], F32)
            self.sdv = sb("sdv", [128, NDC], F32)
            self.bqv = sb("bqv", [128, 3 * NDC], F32)
            self.bov = sb("bov", [128, NDC], F32)
            self.k2m = sb("k2m", [1, NH], F32)
            self.pkm = sb("pkm", [1, NH], F32)
            self.snon = sb("snon", [1, 1], I32)
            self.polldbg = sb("polldbg", [1, 2], I32)
            self.q2m = sb("q2m", [1, NH], F32)
            self.nbh = sb("nbh", [1, NH], F32)
            self.biasT = sb("biasT", [128, NH * (NKC + 1)], F32)
            self.mrow = sb("mrow", [1, NKC], F32)
            self.cv_stack = top.enter_context(contextlib.ExitStack())
            csb = lambda n, s, d: self.cv_stack.enter_context(
                nc.sbuf_tensor(self._un(n), s, d))
            self.cv_i = [csb(f"cvi{i}", [128, 512], I32) for i in range(4)]
            self.cv_f = [csb(f"cvf{i}", [128, 1024], F32) for i in range(2)]
            self.cv_o = [csb(f"cvo{i}", [128, 1024], BF16) for i in range(4)]
            self.ps = top.enter_context(nc.psum_tensor("ps", [128, 4096], F32))

            mk = lambda n, s=1: Ctr(top.enter_context(nc.semaphore(n)), s)
            self.dS = mk("dS", 16)
            self.dW = mk("dW", 16)
            self.dH = mk("dH", 16)
            self.dX = mk("dX", 16)
            self.dKV = mk("dKV", 16)
            self.dG = mk("dG", 16)
            self.dC = mk("dC", 16)
            self.dHt = mk("dHt", 16)
            self.dKo = mk("dKo", 16)
            self.dQi = [top.enter_context(nc.semaphore(f"dQi{k}")) for k in range(4)]
            self.dQf = [top.enter_context(nc.semaphore(f"dQf{k}")) for k in range(2)]
            self.eT = mk("eT")
            self.eV = mk("eV")
            self.eA = mk("eA")
            self.eG = mk("eG")

            self.block = top.enter_context(nc.Block())

            self.p_consts()
            self.p_wconv()
            self.p_ffn1()
            def _f(e):
                e.wait_ge(self.eG.sem, self.eG.v)
                e.wait_ge(self.dS.sem, self.dS.v)
                e.wait_ge(self.dG.sem, self.dG.v)
            for eng in ("sync", "gpsimd", "vector", "scalar", "tensor"):
                getattr(self.block, eng)(_f)
            self.cv_stack.close()
            with contextlib.ExitStack() as mid:
                msb = lambda n, s, d: mid.enter_context(nc.sbuf_tensor(self._un(n), s, d))
                self.QT = msb("QT", [128, NH * OWN], BF16)
                self.p_qkv()
                self.kv_ready_dKo = self.dKo.v
                self.p_norms()
                self.p_exchange()
                self.p_bias()
                self.AT = msb("AT", [128, NDC * OWN], BF16)
                self.p_attention()
                self.p_outproj()
            self.p_ffn3()
            self.p_final()
        return nc

    def bank(self, b, w=512, p=128):
        return self.ps[0:p, 512 * b:512 * b + w]

    def fence(self, light=False):
        if light:
            cs = [self.eT, self.eV, self.eA]
        else:
            cs = [self.eT, self.eV, self.eA,
                  self.dW, self.dH, self.dX, self.dKV, self.dG,
                  self.dC, self.dHt, self.dKo]
        vals = [(c.sem, c.v) for c in cs]
        for eng in ("sync", "tensor", "vector", "scalar", "gpsimd"):
            def _f(e, vals=vals):
                for sem, v in vals:
                    if v:
                        e.wait_ge(sem, v)
            getattr(self.block, eng)(_f)

    def p_consts(self):
        dS, eV = self.dS, self.eV

        def _s(s):
            for dst, src in [(self.ident, self.identin), (self.ones, self.onesin),
                             (self.o1x128, self.onesk1), (self.mrow, self.maskrow),
                             (self.snon, self.noncex)]:
                dS.inc(s.dma_start(out=dst[:, :], in_=src[:, :]))
            for dst, srcp in [
                (self.sgv, self.sgx), (self.suv, self.sux), (self.sdv, self.sdx),
                (self.bqv, self.bqkv), (self.bov, self.boutx),
            ]:
                nch = dst.shape[1]
                dS.inc(s.dma_start(out=sb3(dst[:, :], nch), in_=dram3(srcp),
                                   allow_slow_non_contiguous=True))
        self.block.sync(_s)

        def _v(v):
            v.wait_ge(dS.sem, dS.v)
            v.tensor_scalar_mul(self.sdv[:, :], self.sdv[:, :], 0.5)
            v.memset(self.k2m[:, :], 0.0)
            eV.inc(v.memset(self.q2m[:, :], 0.0))
        self.block.vector(_v)

    def p_wconv(self):
        dS, dG, eG = self.dS, self.dG, self.eG
        jobs = []
        for hc in range(NHC):
            for c0 in (0, 512, 1024, 1536):
                jobs.append((self.wqg, self.WgB, hc * 128, c0, 512, I32))
                jobs.append((self.wqu, self.WuB, hc * 128, c0, 512, I32))
        n_gu = len(jobs)
        for dc in range(NDC):
            for c0 in range(0, HID, 512):
                jobs.append((self.wqd, self.WdB, dc * 128, c0, 512, I32))
        n_wqd = len(jobs)
        for dc in range(NDC):
            for c0 in range(0, 3 * D, 1024):
                jobs.append((self.wqkv, self.WqkvB, dc * 128, c0, 1024, F32))
        n_wqkv = len(jobs)
        for dc in range(NDC):
            for c0 in (0, 1024):
                jobs.append((self.wout, self.WoutB, dc * 128, c0, 1024, F32))

        in_slot = []
        dg_after = []
        eG0 = eG.v
        nj = len(jobs)
        ci = [0, 0, 0, 0]
        cf = [0, 0]

        def _s(s):
            nf = 0
            for ji, (srcp, dst, row, c0, w, dt) in enumerate(jobs):
                if dt == I32:
                    if ji >= 4:
                        s.wait_ge(eG.sem, eG0 + ji - 3)
                    stg = self.cv_i[ji % 4]
                    sem = self.dQi[ji % 4]
                    ci[ji % 4] += 16
                    cnt = ci[ji % 4]
                else:
                    if nf >= 2:
                        s.wait_ge(eG.sem, eG0 + ji - 1)
                    stg = self.cv_f[nf % 2]
                    sem = self.dQf[nf % 2]
                    cf[nf % 2] += 16
                    cnt = cf[nf % 2]
                    nf += 1
                s.dma_start(out=stg[:, 0:w],
                            in_=srcp[row:row + 128, c0:c0 + w]).then_inc(sem, 16)
                in_slot.append((sem, cnt))
        self.block.sync(_s)

        def _g(g):
            for ji, (srcp, dst, row, c0, w, dt) in enumerate(jobs):
                g.wait_ge(eG.sem, eG0 + ji + 1)
                dG.inc(g.dma_start(out=dst[row:row + 128, c0:c0 + w],
                                   in_=self.cv_o[ji % 4][:, 0:w]))
                dg_after.append(dG.v)
        self.block.gpsimd(_g)

        def _v(v):
            nf = 0
            for ji, (srcp, dst, row, c0, w, dt) in enumerate(jobs):
                sem, cnt = in_slot[ji]
                v.wait_ge(sem, cnt)
                if ji >= 4:
                    v.wait_ge(dG.sem, dg_after[ji - 4])
                if dt == I32:
                    stg = self.cv_i[ji % 4]
                else:
                    stg = self.cv_f[nf % 2]
                    nf += 1
                if dt == I32:
                    eG.inc(v.tensor_scalar(out=self.cv_o[ji % 4][:, 0:w],
                                           in0=stg[:, 0:w], scalar1=128.0,
                                           scalar2=None, op0=OP.subtract))
                else:
                    eG.inc(v.tensor_copy(out=self.cv_o[ji % 4][:, 0:w],
                                         in_=stg[:, 0:w]))
        self.block.vector(_v)

        self.wcv_slab = {hs: dg_after[32 * (hs + 1) - 1] for hs in range(NHC // 4)}
        self.wcv_wqd = dg_after[n_wqd - 1]
        self.wcv_wqkv = dg_after[n_wqkv - 1]
        self.wcv_wout = dg_after[len(jobs) - 1]
        self.wconv_dG = dG.v

    def mk_ln(self, ph, tw=1024):
        sb = lambda n, s, d: ph.enter_context(self.nc.sbuf_tensor(self._un(n), s, d))
        return dict(
            tw=tw,
            xb=sb("ln_xb", [128, NDC * tw], BF16),
            sq=sb("ln_sq", [128, NDC * tw], BF16),
            m=sb("ln_m", [1, tw], F32),
            r=sb("ln_r", [1, tw], F32),
            t1=sb("ln_t1", [1, tw], F32),
            mb=sb("ln_mb", [1, tw], BF16),
            rb=sb("ln_rb", [1, tw], BF16),
            reb=sb("ln_reb", [1, tw], BF16),
            msb=sb("ln_msb", [128, tw], BF16),
            rsb=sb("ln_rsb", [128, tw], BF16),
        )

    def emit_ln(self, dst, dst_base, dst_stride, ln, tw=1024):
        blk = self.block
        eT, eV, eA = self.eT, self.eV, self.eA
        n = NDC
        xb = ln["xb"]
        nh = tw // 512

        def _v(v):
            v.wait_ge(self.dX.sem, self.dX.v)
            v.wait_ge(self.dC.sem, self.dC.v)
            eV.inc(v.tensor_tensor(ln["sq"][:, 0:n * tw], xb[:, 0:n * tw],
                                   xb[:, 0:n * tw], op=OP.mult))
        blk.vector(_v)

        def _t(t):
            t.wait_ge(eV.sem, eV.v)
            for dc in range(n):
                for h in range(nh):
                    t.matmul(self.ps[0:1, h * 512:(h + 1) * 512], self.ones[:, :],
                             xb[:, dc * tw + h * 512:dc * tw + (h + 1) * 512],
                             start=(dc == 0), stop=(dc == n - 1))
            ins = None
            for dc in range(n):
                for h in range(nh):
                    ins = t.matmul(self.ps[0:1, 1024 + h * 512:1024 + (h + 1) * 512],
                                   self.ones[:, :],
                                   ln["sq"][:, dc * tw + h * 512:dc * tw + (h + 1) * 512],
                                   start=(dc == 0), stop=(dc == n - 1))
            eT.inc(ins)
        blk.tensor(_t)

        def _v(v):
            v.wait_ge(eT.sem, eT.v)
            v.tensor_scalar_mul(ln["m"][0:1, 0:tw], self.ps[0:1, 0:tw], 1.0 / D)
            v.tensor_scalar_mul(ln["r"][0:1, 0:tw], self.ps[0:1, 1024:1024 + tw], 1.0 / D)
            v.tensor_tensor(ln["t1"][0:1, 0:tw], ln["m"][0:1, 0:tw],
                            ln["m"][0:1, 0:tw], op=OP.mult)
            v.tensor_tensor(ln["r"][0:1, 0:tw], ln["r"][0:1, 0:tw],
                            ln["t1"][0:1, 0:tw], op=OP.subtract)
            eV.inc(v.tensor_scalar_add(ln["r"][0:1, 0:tw], ln["r"][0:1, 0:tw], 1e-5))
        blk.vector(_v)

        def _a(a):
            a.wait_ge(eV.sem, eV.v)
            eA.inc(a.activation(out=ln["t1"][0:1, 0:tw], in_=ln["r"][0:1, 0:tw],
                                func=AF.Sqrt))
        blk.scalar(_a)

        def _v(v):
            v.wait_ge(eA.sem, eA.v)
            v.reciprocal(ln["r"][0:1, 0:tw], ln["t1"][0:1, 0:tw])
            v.tensor_copy(out=ln["rb"][0:1, 0:tw], in_=ln["r"][0:1, 0:tw])
            v.tensor_copy(out=ln["t1"][0:1, 0:tw], in_=ln["rb"][0:1, 0:tw])
            v.tensor_tensor(ln["t1"][0:1, 0:tw], ln["r"][0:1, 0:tw],
                            ln["t1"][0:1, 0:tw], op=OP.subtract)
            v.tensor_copy(out=ln["reb"][0:1, 0:tw], in_=ln["t1"][0:1, 0:tw])
            eV.inc(v.tensor_copy(out=ln["mb"][0:1, 0:tw], in_=ln["m"][0:1, 0:tw]))
        blk.vector(_v)

        def _t(t):
            t.wait_ge(eV.sem, eV.v)
            ins = None
            for h in range(nh):
                t.matmul(self.ps[:, h * 512:(h + 1) * 512], self.o1x128[:, :],
                         ln["mb"][0:1, h * 512:(h + 1) * 512], start=True, stop=True)
                t.matmul(self.ps[:, 1024 + h * 512:1024 + (h + 1) * 512],
                         self.o1x128[:, :], ln["rb"][0:1, h * 512:(h + 1) * 512],
                         start=True, stop=False)
                ins = t.matmul(self.ps[:, 1024 + h * 512:1024 + (h + 1) * 512],
                               self.o1x128[:, :], ln["reb"][0:1, h * 512:(h + 1) * 512],
                               start=False, stop=True)
            eT.inc(ins)
        blk.tensor(_t)

        def _v(v):
            v.wait_ge(eT.sem, eT.v)
            v.tensor_copy(out=ln["msb"][:, 0:tw], in_=self.ps[:, 0:tw])
            eV.inc(v.tensor_copy(out=ln["rsb"][:, 0:tw], in_=self.ps[:, 1024:1024 + tw]))
        blk.vector(_v)

        def _v(v):
            for dc in range(n):
                v.tensor_tensor(ln["sq"][:, dc * tw:(dc + 1) * tw],
                                xb[:, dc * tw:(dc + 1) * tw],
                                ln["msb"][:, 0:tw], op=OP.subtract)
            ins = None
            for dc in range(n):
                o = dst_base + dc * dst_stride
                ins = v.tensor_tensor(dst[:, o:o + tw],
                                      ln["sq"][:, dc * tw:(dc + 1) * tw],
                                      ln["rsb"][:, 0:tw], op=OP.mult)
            eV.inc(ins)
        blk.vector(_v)

    def p_ffn1(self):
        nc, blk = self.nc, self.block
        dW, dH, dX, dG = self.dW, self.dH, self.dX, self.dG
        eT, eV, eA = self.eT, self.eV, self.eA

        for q in range(NQL):
            if q > 0:
                self.fence()
            with contextlib.ExitStack() as ph:
                sb = lambda n, s, d: ph.enter_context(nc.sbuf_tensor(self._un(n), s, d))
                gph = ph.enter_context(contextlib.ExitStack())
                gsb = lambda n, s, d: gph.enter_context(nc.sbuf_tensor(self._un(n), s, d))
                xn = gsb("xn1", [128, NDC * Q4], BF16)

                with contextlib.ExitStack() as lph:
                    ln = self.mk_ln(lph)
                    for tt2 in range(2):
                        tg = q * Q4 + tt2 * 1024

                        def _g(g, tg=tg):
                            g.wait_ge(eV.sem, eV.v)
                            self.dC.inc(g.dma_start(
                                out=sb3(ln["xb"][:, :], NDC),
                                in_=dram3(self.xT[:, tg:tg + 1024])))
                        blk.gpsimd(_g)
                        self.emit_ln(xn, tt2 * 1024, Q4, ln)

                if self.debug:
                    def _g(g, q=q):
                        g.wait_ge(eV.sem, eV.v)
                        dG.inc(g.dma_start(
                            out=dram3(self.dbg["dXN1"][:, q * Q4:(q + 1) * Q4]),
                            in_=sb3(xn[:, :], NDC)))
                    blk.gpsimd(_g)

                self.fence(light=True)
                wgT = [gsb(f"wgT{i}", [128, NDC * 512], BF16) for i in range(2)]
                wuT = [gsb(f"wuT{i}", [128, NDC * 512], BF16) for i in range(2)]
                gt = [gsb(f"gt{i}", [128, 512], BF16) for i in range(2)]
                ht = [gsb(f"ht{i}", [128, 512], BF16) for i in range(4)]
                ht_dma = [0, 0, 0, 0]
                gidx = 0
                slab_pe_done = {}
                prev_dve = [eV.v, eV.v]
                prev_act = [eA.v, eA.v]
                slab_loaded = {}

                for hs in range(NHC // 4):
                    b = hs % 2

                    def _s(s, hs=hs, b=b):
                        if hs >= 2:
                            s.wait_ge(eT.sem, slab_pe_done[hs - 2])
                        if q == 0:
                            s.wait_ge(dG.sem, self.wcv_slab[hs])
                        for dc in range(NDC):
                            dW.inc(s.dma_start_transpose(
                                out=wgT[b][:, dc * 512:(dc + 1) * 512],
                                in_=self.WgB[hs * 512:(hs + 1) * 512,
                                             dc * 128:(dc + 1) * 128]))
                        for dc in range(NDC):
                            dW.inc(s.dma_start_transpose(
                                out=wuT[b][:, dc * 512:(dc + 1) * 512],
                                in_=self.WuB[hs * 512:(hs + 1) * 512,
                                             dc * 128:(dc + 1) * 128]))
                    blk.sync(_s)
                    slab_loaded[hs] = dW.v

                    for hcw in range(4):
                        hc = hs * 4 + hcw
                        for tt in range(4):
                            par = (hcw * 4 + tt) % 2
                            gb, ub = (4, 5) if par == 0 else (6, 7)

                            def _t(t, b=b, hcw=hcw, tt=tt, par=par, gb=gb, ub=ub,
                                   first=(hcw == 0 and tt == 0), hs=hs):
                                if first:
                                    t.wait_ge(dW.sem, slab_loaded[hs])
                                    if hs == 0:
                                        t.wait_ge(eV.sem, eV.v)
                                t.wait_ge(eV.sem, prev_dve[par])
                                t.wait_ge(eA.sem, prev_act[par])
                                for dc in range(NDC):
                                    t.matmul(self.bank(gb),
                                             wgT[b][:, dc * 512 + hcw * 128:
                                                    dc * 512 + hcw * 128 + 128],
                                             xn[:, dc * Q4 + tt * 512:
                                                dc * Q4 + (tt + 1) * 512],
                                             start=(dc == 0), stop=(dc == NDC - 1))
                                ins = None
                                for dc in range(NDC):
                                    ins = t.matmul(self.bank(ub),
                                                   wuT[b][:, dc * 512 + hcw * 128:
                                                          dc * 512 + hcw * 128 + 128],
                                                   xn[:, dc * Q4 + tt * 512:
                                                      dc * Q4 + (tt + 1) * 512],
                                                   start=(dc == 0), stop=(dc == NDC - 1))
                                eT.inc(ins)
                            blk.tensor(_t)
                            slab_pe_done[hs] = eT.v

                            gpar = gidx % 2
                            hpar = gidx % 4

                            def _a(a, hc=hc, gb=gb, gpar=gpar, gidx=gidx):
                                a.wait_ge(eT.sem, eT.v)
                                if gidx >= 2:
                                    a.wait_ge(eV.sem, prev_dve[(gidx - 2) % 2])
                                eA.inc(a.activation(out=gt[gpar][:, :],
                                                    in_=self.bank(gb),
                                                    func=AF.Silu,
                                                    scale=self.sgv[:, hc:hc + 1]))
                            blk.scalar(_a)
                            prev_act[par] = eA.v

                            def _v(v, hc=hc, ub=ub, gpar=gpar, hpar=hpar):
                                v.wait_ge(eA.sem, eA.v)
                                v.wait_ge(self.dHt.sem, ht_dma[hpar])
                                eV.inc(v.scalar_tensor_tensor(
                                    out=ht[hpar][:, :], in0=self.bank(ub),
                                    scalar=self.suv[:, hc:hc + 1], in1=gt[gpar][:, :],
                                    op0=OP.mult, op1=OP.mult))
                            blk.vector(_v)
                            prev_dve[par] = eV.v

                            def _g(g, hc=hc, tt=tt, hpar=hpar):
                                g.wait_ge(eV.sem, eV.v)
                                self.dHt.inc(g.dma_start(
                                    out=self.HT1[hc * 128:(hc + 1) * 128,
                                                 tt * 512:(tt + 1) * 512],
                                    in_=ht[hpar][:, :]))
                                ht_dma[hpar] = self.dHt.v
                            blk.gpsimd(_g)
                            gidx += 1
                gu_done_T = eT.v
                gu_done_G = self.dHt.v
                if self.debug and q == 0:
                    def _s(s):
                        s.wait_ge(dG.sem, gu_done_G)
                        self.dS.inc(s.dma_start(out=self.dbg["dHT"][:, :],
                                                in_=self.HT1[:, :]))
                    blk.sync(_s)
                gph.close()
                self.fence()

                wdT = sb("wdT", [128, NHC * 1024], BF16)
                hbig = sb("hbig", [128, NHC * 512], BF16)
                xdn = sb("xdn", [128, 8 * 512], F32)
                y1t = sb("y1t", [128, 8 * 512], F32)
                y1tb = sb("y1tb", [128, 8 * 512], BF16)

                for dh in range(2):
                    wd_tick = {}

                    def _s(s, dh=dh):
                        if dh == 0:
                            s.wait_ge(self.dHt.sem, gu_done_G)
                            if q == 0:
                                s.wait_ge(dG.sem, self.wcv_wqd)
                        else:
                            s.wait_ge(eT.sem, eT.v)
                        for hc in range(NHC):
                            for ds2 in range(2):
                                dW.inc(s.dma_start_transpose(
                                    out=wdT[:, hc * 1024 + ds2 * 512:
                                            hc * 1024 + (ds2 + 1) * 512],
                                    in_=self.WdB[dh * 1024 + ds2 * 512:
                                                 dh * 1024 + (ds2 + 1) * 512,
                                                 hc * 128:(hc + 1) * 128]))
                            wd_tick[hc] = dW.v
                    blk.sync(_s)

                    for tt in range(4):
                        h_tick = {}

                        def _s(s, dh=dh, tt=tt):
                            s.wait_ge(eT.sem, eT.v)
                            s.wait_ge(eV.sem, eV.v)
                            for piece in range(4):
                                hc0 = piece * 11
                                dH.inc(s.dma_start(
                                    out=sb3(hbig[:, hc0 * 512:(hc0 + 11) * 512], 11),
                                    in_=dram3(self.HT1[hc0 * 128:(hc0 + 11) * 128,
                                                       tt * 512:(tt + 1) * 512])))
                                h_tick[piece] = dH.v
                            dX.inc(s.dma_start(
                                out=sb3(xdn[:, :], 8),
                                in_=dram3(self.xT[dh * 1024:(dh + 1) * 1024,
                                                  q * Q4 + tt * 512:
                                                  q * Q4 + (tt + 1) * 512])))
                        blk.sync(_s)

                        def _t(t):
                            t.wait_ge(eV.sem, eV.v)
                            ins = None
                            for hc in range(NHC):
                                if hc % 11 == 0:
                                    t.wait_ge(dH.sem, h_tick[hc // 11])
                                t.wait_ge(dW.sem, wd_tick[hc])
                                for dcw in range(8):
                                    ins = t.matmul(
                                        self.bank(dcw),
                                        wdT[:, hc * 1024 + dcw * 128:
                                            hc * 1024 + dcw * 128 + 128],
                                        hbig[:, hc * 512:(hc + 1) * 512],
                                        start=(hc == 0), stop=(hc == NHC - 1))
                            eT.inc(ins)
                        blk.tensor(_t)

                        own = (q == 0 and tt < 2)

                        def _v(v, own=own):
                            v.wait_ge(eT.sem, eT.v)
                            v.wait_ge(self.dX.sem, self.dX.v)
                            v.wait_ge(dG.sem, dG.v)
                            dst = y1t if own else y1tb
                            ins = None
                            for dcw in range(8):
                                dc = dh * 8 + dcw
                                ins = v.scalar_tensor_tensor(
                                    out=dst[:, dcw * 512:(dcw + 1) * 512],
                                    in0=self.bank(dcw),
                                    scalar=self.sdv[:, dc:dc + 1],
                                    in1=xdn[:, dcw * 512:(dcw + 1) * 512],
                                    op0=OP.mult, op1=OP.add)
                            eV.inc(ins)
                        blk.vector(_v)

                        def _g(g, q=q, dh=dh, tt=tt, own=own):
                            g.wait_ge(eV.sem, eV.v)
                            tg = q * Q4 + tt * 512
                            if own:
                                dG.inc(g.dma_start(
                                    out=dram3(self.Y1OWN[dh * 1024:(dh + 1) * 1024,
                                                         tt * 512:(tt + 1) * 512]),
                                    in_=sb3(y1t[:, :], 8)))
                            else:
                                dG.inc(g.dma_start(
                                    out=dram3(self.Y1OTH[dh * 1024:(dh + 1) * 1024,
                                                         tg - 1024:tg - 512]),
                                    in_=sb3(y1tb[:, :], 8)))
                        blk.gpsimd(_g)

        if self.debug:
            def _s(s):
                s.wait_ge(self.dG.sem, self.dG.v)
                self.dS.inc(s.dma_start(out=self.dbg["dY1"][:, :], in_=self.Y1OWN[:, :]))
                self.dS.inc(s.dma_start(out=self.dbg["dY1O"][:, :], in_=self.Y1OTH[:, :]))
            self.block.sync(_s)

    @staticmethod
    def _dc(dh, dcw):
        return dh * 8 + dcw

    def p_qkv(self):
        nc, blk = self.nc, self.block
        dS, dW, dX, dG = self.dS, self.dW, self.dX, self.dG
        eT, eV, eA = self.eT, self.eV, self.eA

        for q in range(NQL):
            self.fence()
            with contextlib.ExitStack() as ph:
                sb = lambda n, s, d: ph.enter_context(nc.sbuf_tensor(self._un(n), s, d))
                xn = sb("xn2", [128, NDC * Q4], BF16)

                with contextlib.ExitStack() as lph:
                    ln = self.mk_ln(lph)
                    for tt2 in range(2):
                        tg = q * Q4 + tt2 * 1024
                        own = (q == 0 and tt2 == 0)

                        def _g(g, tg=tg, own=own, q=q, tt2=tt2):
                            g.wait_ge(eV.sem, eV.v)
                            if q == 0 and tt2 == 0:
                                g.wait_ge(dG.sem, dG.v)
                            if own:
                                self.dC.inc(g.dma_start(
                                    out=sb3(ln["xb"][:, :], NDC),
                                    in_=dram3(self.Y1OWN[:, :])))
                            else:
                                self.dC.inc(g.dma_start(
                                    out=sb3(ln["xb"][:, :], NDC),
                                    in_=dram3(self.Y1OTH[:, tg - 1024:tg])))
                        blk.gpsimd(_g)
                        self.emit_ln(xn, tt2 * 1024, Q4, ln)

                self.fence(light=True)
                wkv = [sb(f"wkv{i}", [128, NDC * 128], BF16) for i in range(2)]
                ktile = [sb(f"ktile{i}", [128, 512], BF16) for i in range(2)]
                vtt = [sb(f"vtt{i}", [128, 512], BF16) for i in range(2)]
                kt_dma = [self.dKo.v, self.dKo.v]
                kt_cons = [eT.v, eT.v]
                vt_dma = [self.dKo.v, self.dKo.v]
                tidx = 0

                jlist = list(range(16, 48)) + (list(range(0, 16)) if q == 0 else [])
                wk_pe_done = {}
                for idx, jc in enumerate(jlist):
                    b = idx % 2

                    def _s(s, jc=jc, b=b, idx=idx):
                        if idx >= 2:
                            s.wait_ge(eT.sem, wk_pe_done[idx - 2])
                        if q == 0 and idx == 0:
                            s.wait_ge(dG.sem, self.wcv_wqkv)
                        dW.inc(s.dma_start(
                            out=sb3(wkv[b][:, :], NDC),
                            in_=dram3(self.WqkvB[:, jc * 128:(jc + 1) * 128])))
                    blk.sync(_s)
                    w_loaded = dW.v

                    nt = 4 if jc >= 16 else 2
                    for tt in range(nt):
                        kb = 4 + tidx % 2
                        kpar = tidx % 2

                        def _t(t, b=b, tt=tt, kb=kb, w_loaded=w_loaded,
                               first=(tt == 0)):
                            if first:
                                t.wait_ge(dW.sem, w_loaded)
                            t.wait_ge(eV.sem, eV.v)
                            ins = None
                            for dc in range(NDC):
                                ins = t.matmul(self.bank(kb),
                                               wkv[b][:, dc * 128:(dc + 1) * 128],
                                               xn[:, dc * Q4 + tt * 512:
                                                  dc * Q4 + (tt + 1) * 512],
                                               start=(dc == 0), stop=(dc == NDC - 1))
                            eT.inc(ins)
                        blk.tensor(_t)
                        wk_pe_done[idx] = eT.v

                        def _v(v, jc=jc, kb=kb, kpar=kpar):
                            v.wait_ge(eT.sem, eT.v)
                            v.wait_ge(self.dKo.sem, kt_dma[kpar])
                            v.wait_ge(eT.sem, kt_cons[kpar])
                            eV.inc(v.tensor_scalar(out=ktile[kpar][:, :],
                                                   in0=self.bank(kb),
                                                   scalar1=self.bqv[:, jc:jc + 1],
                                                   scalar2=None, op0=OP.add))
                        blk.vector(_v)

                        tg = q * Q4 + tt * 512
                        if 16 <= jc < 32:
                            def _g(g, jc=jc, tg=tg, kpar=kpar):
                                g.wait_ge(eV.sem, eV.v)
                                self.dKo.inc(g.dma_start(
                                    out=self.KT[(jc - 16) * 128:(jc - 15) * 128,
                                                tg:tg + 512],
                                    in_=ktile[kpar][:, :]))
                                kt_dma[kpar] = self.dKo.v
                            blk.gpsimd(_g)
                        elif jc >= 32:
                            h = jc - 32
                            tb = 6 + tidx % 2

                            def _t(t, kpar=kpar, tb=tb):
                                t.wait_ge(eV.sem, eV.v)
                                ins = None
                                for i in range(4):
                                    ins = t.matmul(
                                        self.ps[:, tb * 512 + i * 128:
                                                tb * 512 + (i + 1) * 128],
                                        ktile[kpar][:, i * 128:(i + 1) * 128],
                                        self.ident[:, :], start=True, stop=True)
                                eT.inc(ins)
                            blk.tensor(_t)
                            kt_cons[kpar] = eT.v

                            def _v(v, kpar=kpar, tb=tb):
                                v.wait_ge(eT.sem, eT.v)
                                v.wait_ge(self.dKo.sem, vt_dma[kpar])
                                eV.inc(v.tensor_copy(out=vtt[kpar][:, :],
                                                     in_=self.ps[:, tb * 512:
                                                                 (tb + 1) * 512]))
                            blk.vector(_v)

                            def _g(g, h=h, tg=tg, kpar=kpar):
                                g.wait_ge(eV.sem, eV.v)
                                self.dKo.inc(g.dma_start(
                                    out=dram3(self.VT[h * T4 + tg:h * T4 + tg + 512, :]),
                                    in_=sb3(vtt[kpar][:, :], 4)))
                                vt_dma[kpar] = self.dKo.v
                            blk.gpsimd(_g)
                        else:
                            h = jc

                            def _v(v, h=h, tt=tt, kpar=kpar):
                                eV.inc(v.tensor_copy(
                                    out=self.QT[:, h * OWN + tt * 512:
                                                h * OWN + (tt + 1) * 512],
                                    in_=ktile[kpar][:, :]))
                            blk.vector(_v)
                        if jc < 16 or jc >= 32:
                            pass
                        tidx += 1

        self.kv_done_dKo = self.dKo.v

        if self.debug:
            def _s(s):
                s.wait_ge(dG.sem, dG.v)
                s.wait_ge(eV.sem, eV.v)
                self.dS.inc(s.dma_start(out=self.dbg["dKT"][:, :], in_=self.KT[:, :]))
                self.dS.inc(s.dma_start(out=self.dbg["dVT"][:, :], in_=self.VT[:, :]))
            blk.sync(_s)

            def _g(g):
                g.wait_ge(eV.sem, eV.v)
                dG.inc(g.dma_start(out=self.dbg["dQT"][:, :], in_=self.QT[:, :]))
            blk.gpsimd(_g)

    def p_exchange(self):
        dKo, eV = self.dKo, self.eV
        v0 = dKo.v
        norms_eV = eV.v

        def _g(g):
            g.wait_ge(dKo.sem, self.kv_ready_dKo)
            for arm in g.switch_core_id():
                p = arm.logical % 2
                g.dma_start(out=self.SHK[p][:, :], in_=self.KT[:, :]
                            ).then_inc(dKo.sem, 16)
                g.dma_start(
                    out=self.SHV[p].rearrange("(r c) b -> r (c b)", c=128),
                    in_=self.VT.rearrange("(r c) b -> r (c b)", c=128),
                ).then_inc(dKo.sem, 16)
                g.wait_ge(eV.sem, norms_eV)
                g.dma_start(out=self.SHKM[p][0:1, :], in_=self.k2m[0:1, :]
                            ).then_inc(dKo.sem, 16)
                g.wait_ge(dKo.sem, v0 + 48)
                g.dma_start(out=self.SHFLG[p][0:1, 0:1], in_=self.snon[0:1, 0:1]
                            ).then_inc(dKo.sem, 16)
                g.wait_ge(dKo.sem, v0 + 64)
                with g.register(f"xfl{int(arm)}") as fl, \
                     g.register(f"xnn{int(arm)}") as nn, \
                     g.register(f"xc{int(arm)}") as c, \
                     g.register(f"xit{int(arm)}") as it, \
                     g.register(f"xtm{int(arm)}") as tm:
                    g.load(nn, self.snon[0:1, 0:1])
                    g.reg_mov(c, 1)
                    g.reg_mov(it, 0)
                    with g.While(c):
                        g.load(fl, self.SHFLG[1 - p][0:1, 0:1])
                        g.reg_alu(c, fl, nn, OP.not_equal)
                        g.reg_add(it, it, 1)
                        g.reg_alu(tm, it, 20000000, OP.is_lt)
                        g.reg_alu(c, c, tm, OP.logical_and)
                    g.reg_save(self.polldbg[0:1, 0:1], it)
                    g.reg_save(self.polldbg[0:1, 1:2], fl)
                g.dma_start(out=self.pkm[0:1, :], in_=self.SHKM[1 - p][0:1, :]
                            ).then_inc(dKo.sem, 16)
        self.block.gpsimd(_g)
        dKo.v = v0 + 80

        def _v(v):
            v.wait_ge(dKo.sem, dKo.v)
            eV.inc(v.tensor_tensor(self.k2m[0:1, :], self.k2m[0:1, :],
                                   self.pkm[0:1, :], op=OP.max))
        self.block.vector(_v)

    def p_norms(self):
        nc, blk = self.nc, self.block
        eT, eV, eA = self.eT, self.eV, self.eA
        self.fence()
        with contextlib.ExitStack() as ph:
            sb = lambda n, s, d: ph.enter_context(nc.sbuf_tensor(self._un(n), s, d))
            kbuf = [sb(f"nk{i}", [128, T4], BF16) for i in range(2)]
            sq = [sb(f"nsq{i}", [128, T4], BF16) for i in range(2)]
            n11 = sb("n11", [1, 1], F32)
            staged = {}
            pe_done = {-1: eT.v, -2: eT.v}

            def _s(s):
                s.wait_ge(self.dKo.sem, self.dKo.v)
                for h in range(2):
                    self.dKV.inc(s.dma_start(out=kbuf[h][:, :],
                                             in_=self.KT[h * 128:(h + 1) * 128, :]))
                    staged[h] = self.dKV.v
            blk.sync(_s)

            for h in range(NH):
                bb = h % 2

                def _a(a, h=h, bb=bb):
                    a.wait_ge(self.dKV.sem, staged[h])
                    a.wait_ge(eT.sem, pe_done.get(h - 2, 0) if h >= 2 else 0)
                    eA.inc(a.activation(out=sq[bb][:, :], in_=kbuf[bb][:, :],
                                        func=AF.Square))
                blk.scalar(_a)

                def _t(t, h=h, bb=bb):
                    t.wait_ge(eA.sem, eA.v)
                    t.wait_ge(eV.sem, eV.v)
                    ins = None
                    for c in range(8):
                        ins = t.matmul(
                            self.ps[0:1, c * 512:(c + 1) * 512], self.ones[:, :],
                            sq[bb][:, c * 512:(c + 1) * 512],
                            start=True, stop=True)
                    eT.inc(ins)
                    pe_done[h] = eT.v
                blk.tensor(_t)

                def _v(v, h=h):
                    v.wait_ge(eT.sem, eT.v)
                    for c in range(8):
                        v.reduce_max(n11[0:1, 0:1], self.ps[0:1, c * 512:(c + 1) * 512],
                                     axis=AX.X)
                        v.tensor_tensor(self.k2m[0:1, h:h + 1], self.k2m[0:1, h:h + 1],
                                        n11[0:1, 0:1], op=OP.max)
                    eV.inc(v.tensor_copy(out=n11[0:1, 0:1], in_=n11[0:1, 0:1]))
                blk.vector(_v)

                if h + 2 < NH:
                    def _s(s, h=h, bb=bb):
                        s.wait_ge(eA.sem, eA.v)
                        self.dKV.inc(s.dma_start(
                            out=kbuf[bb][:, :],
                            in_=self.KT[(h + 2) * 128:(h + 3) * 128, :]))
                        staged[h + 2] = self.dKV.v
                    blk.sync(_s)

            for h in range(NH):
                def _a(a, h=h):
                    a.wait_ge(eV.sem, eV.v)
                    eA.inc(a.activation(out=sq[0][:, 0:OWN],
                                        in_=self.QT[:, h * OWN:(h + 1) * OWN],
                                        func=AF.Square))
                blk.scalar(_a)

                def _t(t, h=h):
                    t.wait_ge(eA.sem, eA.v)
                    t.wait_ge(eV.sem, eV.v)
                    ins = None
                    for half in range(2):
                        ins = t.matmul(self.ps[0:1, half * 512:(half + 1) * 512],
                                       self.ones[:, :],
                                       sq[0][:, half * 512:(half + 1) * 512],
                                       start=True, stop=True)
                    eT.inc(ins)
                blk.tensor(_t)

                def _v(v, h=h):
                    v.wait_ge(eT.sem, eT.v)
                    for half in range(2):
                        v.reduce_max(n11[0:1, 0:1],
                                     self.ps[0:1, half * 512:(half + 1) * 512],
                                     axis=AX.X)
                        v.tensor_tensor(self.q2m[0:1, h:h + 1], self.q2m[0:1, h:h + 1],
                                        n11[0:1, 0:1], op=OP.max)
                    eV.inc(v.tensor_copy(out=n11[0:1, 0:1], in_=n11[0:1, 0:1]))
                blk.vector(_v)

    def p_bias(self):
        nc, blk = self.nc, self.block
        eT, eV, eA = self.eT, self.eV, self.eA

        self.fence()
        with contextlib.ExitStack() as ph:
            sb = lambda n, s, d: ph.enter_context(nc.sbuf_tensor(self._un(n), s, d))
            t16 = sb("t16", [1, NH], F32)
            t16b = sb("t16b", [1, NH], F32)
            rowf = sb("rowf", [1, NKC + 1], F32)
            rowb = sb("rowb", [1, NKC + 1], BF16)

            def _a(a):
                a.wait_ge(eV.sem, eV.v)
                a.activation(out=t16[0:1, :], in_=self.k2m[0:1, :], func=AF.Sqrt)
                eA.inc(a.activation(out=t16b[0:1, :], in_=self.q2m[0:1, :],
                                    func=AF.Sqrt))
            blk.scalar(_a)

            def _v(v):
                v.wait_ge(eA.sem, eA.v)
                v.tensor_tensor(t16[0:1, :], t16[0:1, :], t16b[0:1, :], op=OP.mult)
                eV.inc(v.tensor_scalar_mul(self.nbh[0:1, :], t16[0:1, :], -ISCALE))
            blk.vector(_v)

            for h in range(NH):
                def _v(v, h=h):
                    v.wait_ge(eV.sem, eV.v)
                    v.wait_ge(eT.sem, eT.v)
                    v.tensor_scalar(out=rowf[0:1, 0:NKC], in0=self.mrow[0:1, :],
                                    scalar1=self.nbh[0:1, h:h + 1], scalar2=None,
                                    op0=OP.add)
                    v.tensor_copy(out=rowf[0:1, NKC:NKC + 1],
                                  in_=self.nbh[0:1, h:h + 1])
                    eV.inc(v.tensor_copy(out=rowb[0:1, :], in_=rowf[0:1, :]))
                blk.vector(_v)

                def _t(t, h=h):
                    t.wait_ge(eV.sem, eV.v)
                    eT.inc(t.matmul(self.ps[:, 1024:1024 + NKC + 1],
                                    self.o1x128[:, :], rowb[0:1, :],
                                    start=True, stop=True))
                blk.tensor(_t)

                def _v(v, h=h):
                    v.wait_ge(eT.sem, eT.v)
                    eV.inc(v.tensor_copy(
                        out=self.biasT[:, h * (NKC + 1):(h + 1) * (NKC + 1)],
                        in_=self.ps[:, 1024:1024 + NKC + 1]))
                blk.vector(_v)

        if self.debug:
            def _g(g):
                g.wait_ge(eV.sem, eV.v)
                self.dG.inc(g.dma_start(out=self.dbg["dNB"][:, :], in_=self.nbh[0:1, :]))
                self.dG.inc(g.dma_start(out=self.dbg["dNB2"][0:1, :], in_=self.k2m[0:1, :]))
                self.dG.inc(g.dma_start(out=self.dbg["dNB2"][1:2, :], in_=self.q2m[0:1, :]))
                self.dG.inc(g.dma_start(out=self.dbg["dNB2"][2:3, :], in_=self.pkm[0:1, :]))
                for s_ in range(2):
                    self.dG.inc(g.dma_start(out=self.dbg["dSHKM"][s_:s_ + 1, :],
                                            in_=self.SHKM[s_][0:1, :]))
                    self.dG.inc(g.dma_start(out=self.dbg["dSHFL"][s_:s_ + 1, :],
                                            in_=self.SHFLG[s_][0:1, :]))
                self.dG.inc(g.dma_start(out=self.dbg["dPOLL"][:, :],
                                        in_=self.polldbg[0:1, :]))
            blk.gpsimd(_g)

    def p_attention(self):
        nc, blk = self.nc, self.block
        dKV, dG = self.dKV, self.dG
        eT, eV, eA = self.eT, self.eV, self.eA
        NB = NKC + 1

        self.fence()
        with contextlib.ExitStack() as ph:
            sb = lambda n, s, d: ph.enter_context(nc.sbuf_tensor(self._un(n), s, d))
            kth = [sb(f"kth{i}", [128, T], BF16) for i in range(2)]
            vh = [sb(f"vh{i}", [128, T], BF16) for i in range(2)]
            kdg = [sb(f"kdg{i}", [128, OWN], BF16) for i in range(2)]
            vdg = [sb(f"vdg{i}", [128, OWN], BF16) for i in range(2)]
            pt = [sb(f"pt{i}", [128, OWN], BF16) for i in range(2)]
            dmk = sb("dmk", [128, 8 * OWN], F32)
            rs = sb("rs", [1, OWN], F32)
            rsb = sb("rsb", [1, OWN], BF16)
            rse = sb("rse", [1, OWN], F32)
            rseb = sb("rseb", [1, OWN], BF16)
            rcs = sb("rcs", [128, OWN], F32)

            def _s(s):
                self.dS.inc(s.dma_start(
                    out=sb3(dmk[:, :], 8),
                    in_=dram3(self.diagmask[:, :])))
            blk.sync(_s)

            head_pe_done = {-1: eT.v, -2: eT.v}
            staged = {}

            def stage(s, h):
                b = h % 2
                if h >= 2:
                    s.wait_ge(eT.sem, head_pe_done[h - 2])
                if h == 0:
                    s.wait_ge(self.dKo.sem, self.dKo.v)
                for sl in range(2):
                    dKV.inc(s.dma_start(
                        out=kth[b][:, sl * T4:(sl + 1) * T4],
                        in_=self.SHK[sl][h * 128:(h + 1) * 128, :]))
                    for j in range(2):
                        r0 = h * T4 + j * Q4
                        dKV.inc(s.dma_start(
                            out=sb3(vh[b][:, sl * T4 + j * Q4:sl * T4 + (j + 1) * Q4], 16),
                            in_=dram3(self.SHV[sl][r0:r0 + Q4, :])))
                dKV.inc(s.dma_start(out=kdg[b][:, :],
                                    in_=self.KT[h * 128:(h + 1) * 128, 0:OWN]))
                dKV.inc(s.dma_start(out=sb3(vdg[b][:, :], 8),
                                    in_=dram3(self.VT[h * T4:h * T4 + OWN, :])))
                staged[h] = dKV.v

            def _s(s):
                stage(s, 0)
            blk.sync(_s)

            for h in range(NH):
                b = h % 2
                eA0 = eA.v
                eV0 = eV.v
                NIT = NKC + 8

                def srcs(i, b):
                    if i < NKC:
                        return (kth[b][:, i * 128:(i + 1) * 128],
                                vh[b][:, i * 128:(i + 1) * 128], None, i)
                    d = i - NKC
                    return (kdg[b][:, d * 128:(d + 1) * 128],
                            vdg[b][:, d * 128:(d + 1) * 128], d, NKC)

                def _t(t, h=h, b=b, eA0=eA0, eV0=eV0):
                    t.wait_ge(dKV.sem, staged[h])
                    t.wait_ge(eV.sem, eV0)

                    def emit_s(i):
                        ksrc, _, _, _ = srcs(i, b)
                        sb_ = (i % 2) * 2
                        t.matmul(self.ps[:, sb_ * 512:sb_ * 512 + 512],
                                 ksrc, self.QT[:, h * OWN:h * OWN + 512],
                                 start=True, stop=True)
                        ins = t.matmul(self.ps[:, (sb_ + 1) * 512:(sb_ + 1) * 512 + 512],
                                       ksrc, self.QT[:, h * OWN + 512:(h + 1) * OWN],
                                       start=True, stop=True)
                        eT.inc(ins)

                    emit_s(0)
                    for i in range(NIT):
                        if i + 1 < NIT:
                            if i >= 1:
                                t.wait_ge(eA.sem, eA0 + i)
                            emit_s(i + 1)
                        _, vsrc, _, _ = srcs(i, b)
                        t.wait_ge(eA.sem, eA0 + i + 1)
                        st = (i == 0)
                        sp = (i == NIT - 1)
                        t.matmul(self.ps[:, 4 * 512:4 * 512 + 512],
                                 vsrc, pt[i % 2][:, 0:512], start=st, stop=sp)
                        t.matmul(self.ps[:, 5 * 512:5 * 512 + 512],
                                 vsrc, pt[i % 2][:, 512:1024], start=st, stop=sp)
                        t.matmul(self.ps[0:1, 6 * 512:6 * 512 + 512],
                                 self.ones[:, :], pt[i % 2][:, 0:512],
                                 start=st, stop=sp)
                        ins = t.matmul(self.ps[0:1, 7 * 512:7 * 512 + 512],
                                       self.ones[:, :], pt[i % 2][:, 512:1024],
                                       start=st, stop=sp)
                        eT.inc(ins)
                    head_pe_done[h] = eT.v
                blk.tensor(_t)
                pe_base = head_pe_done[h] - 2 * (NKC + 8)
                s_done = lambda i, pe_base=pe_base: pe_base + (1 if i == 0 else 2 * i)

                def _v(v, h=h, s_done=s_done):
                    for d in range(8):
                        i = NKC + d
                        sb_ = (i % 2) * 2
                        v.wait_ge(eT.sem, s_done(i))
                        eV.inc(v.tensor_tensor(
                            self.ps[:, sb_ * 512:(sb_ + 2) * 512],
                            self.ps[:, sb_ * 512:(sb_ + 2) * 512],
                            dmk[:, d * OWN:(d + 1) * OWN], op=OP.add))
                blk.vector(_v)

                def _a(a, h=h, s_done=s_done, eV0=eV0):
                    for i in range(NKC + 8):
                        sb_ = (i % 2) * 2
                        if i >= NKC:
                            a.wait_ge(eV.sem, eV0 + (i - NKC) + 1)
                            col = h * NB + NKC
                        else:
                            a.wait_ge(eT.sem, s_done(i))
                            col = h * NB + i
                        eA.inc(a.activation(out=pt[i % 2][:, :],
                                            in_=self.ps[:, sb_ * 512:(sb_ + 2) * 512],
                                            func=AF.Exp,
                                            bias=self.biasT[:, col:col + 1],
                                            scale=ISCALE))
                blk.scalar(_a)

                def _v(v, h=h):
                    v.wait_ge(eT.sem, head_pe_done[h])
                    v.reciprocal(rs[0:1, :], self.ps[0:1, 6 * 512:8 * 512])
                    v.tensor_copy(out=rsb[0:1, :], in_=rs[0:1, :])
                    v.tensor_copy(out=rse[0:1, :], in_=rsb[0:1, :])
                    v.tensor_tensor(rse[0:1, :], rs[0:1, :], rse[0:1, :],
                                    op=OP.subtract)
                    eV.inc(v.tensor_copy(out=rseb[0:1, :], in_=rse[0:1, :]))
                blk.vector(_v)

                def _t(t, h=h):
                    t.wait_ge(eV.sem, eV.v)
                    t.matmul(self.ps[:, 0:512], self.o1x128[:, :], rsb[0:1, 0:512],
                             start=True, stop=False)
                    t.matmul(self.ps[:, 0:512], self.o1x128[:, :], rseb[0:1, 0:512],
                             start=False, stop=True)
                    t.matmul(self.ps[:, 512:1024], self.o1x128[:, :],
                             rsb[0:1, 512:1024], start=True, stop=False)
                    eT.inc(t.matmul(self.ps[:, 512:1024], self.o1x128[:, :],
                                    rseb[0:1, 512:1024], start=False, stop=True))
                blk.tensor(_t)

                def _v(v, h=h):
                    v.wait_ge(eT.sem, eT.v)
                    v.tensor_copy(out=rcs[:, :], in_=self.ps[:, 0:1024])
                    eV.inc(v.tensor_tensor(self.AT[:, h * OWN:(h + 1) * OWN],
                                           self.ps[:, 4 * 512:6 * 512],
                                           rcs[:, :], op=OP.mult))
                blk.vector(_v)

                if h + 1 < NH:
                    def _s(s, h=h):
                        stage(s, h + 1)
                    blk.sync(_s)

        if self.debug:
            def _g(g):
                g.wait_ge(eV.sem, eV.v)
                dG.inc(g.dma_start(out=self.dbg["dAT"][:, :], in_=self.AT[:, :]))
            blk.gpsimd(_g)

    def p_outproj(self):
        nc, blk = self.nc, self.block
        dW, dX, dG = self.dW, self.dX, self.dG
        eT, eV = self.eT, self.eV

        self.fence()
        with contextlib.ExitStack() as ph:
            sb = lambda n, s, d: ph.enter_context(nc.sbuf_tensor(self._un(n), s, d))
            wo = [sb(f"wo{i}", [128, NDC * 128], BF16) for i in range(2)]
            y1o = sb("y1o", [128, OWN], F32)
            y2t = sb("y2t", [128, OWN], F32)
            oc_pe_done = {}

            for oc in range(NDC):
                b = oc % 2

                def _s(s, oc=oc, b=b):
                    if oc >= 2:
                        s.wait_ge(eT.sem, oc_pe_done[oc - 2])
                    else:
                        s.wait_ge(dG.sem, self.wcv_wout)
                    dW.inc(s.dma_start(
                        out=sb3(wo[b][:, :], NDC),
                        in_=dram3(self.WoutB[:, oc * 128:(oc + 1) * 128])))
                    s.wait_ge(eV.sem, eV.v)
                    dX.inc(s.dma_start(
                        out=y1o[:, :],
                        in_=self.Y1OWN[oc * 128:(oc + 1) * 128, :]))
                blk.sync(_s)
                w_loaded = dW.v
                x_loaded = dX.v

                par = oc % 2
                pb = 0 if par == 0 else 2

                def _t(t, b=b, pb=pb, w_loaded=w_loaded):
                    t.wait_ge(dW.sem, w_loaded)
                    t.wait_ge(eV.sem, eV.v)
                    for dc in range(NDC):
                        t.matmul(self.ps[:, pb * 512:pb * 512 + 512],
                                 wo[b][:, dc * 128:(dc + 1) * 128],
                                 self.AT[:, dc * OWN:dc * OWN + 512],
                                 start=(dc == 0), stop=(dc == NDC - 1))
                    ins = None
                    for dc in range(NDC):
                        ins = t.matmul(self.ps[:, (pb + 1) * 512:(pb + 1) * 512 + 512],
                                       wo[b][:, dc * 128:(dc + 1) * 128],
                                       self.AT[:, dc * OWN + 512:(dc + 1) * OWN],
                                       start=(dc == 0), stop=(dc == NDC - 1))
                    eT.inc(ins)
                blk.tensor(_t)
                oc_pe_done[oc] = eT.v

                def _v(v, oc=oc, pb=pb, x_loaded=x_loaded):
                    v.wait_ge(eT.sem, eT.v)
                    v.wait_ge(dX.sem, x_loaded)
                    v.wait_ge(dG.sem, dG.v)
                    eV.inc(v.scalar_tensor_tensor(
                        out=y2t[:, :], in0=self.ps[:, pb * 512:(pb + 2) * 512],
                        scalar=self.bov[:, oc:oc + 1], in1=y1o[:, :],
                        op0=OP.add, op1=OP.add))
                blk.vector(_v)

                def _g(g, oc=oc):
                    g.wait_ge(eV.sem, eV.v)
                    dG.inc(g.dma_start(out=self.Y2[oc * 128:(oc + 1) * 128, :],
                                       in_=y2t[:, :]))
                blk.gpsimd(_g)

        if self.debug:
            def _s(s):
                s.wait_ge(dG.sem, dG.v)
                self.dS.inc(s.dma_start(out=self.dbg["dY2"][:, :], in_=self.Y2[:, :]))
            blk.sync(_s)

    def p_ffn3(self):
        nc, blk = self.nc, self.block
        dW, dH, dX, dG = self.dW, self.dH, self.dX, self.dG
        eT, eV, eA = self.eT, self.eV, self.eA

        self.fence()
        with contextlib.ExitStack() as ph:
            sb = lambda n, s, d: ph.enter_context(nc.sbuf_tensor(self._un(n), s, d))
            h3 = sb("h3", [128, NHC * OWN], BF16)

            with contextlib.ExitStack() as gph:
                gsb = lambda n, s, d: gph.enter_context(nc.sbuf_tensor(self._un(n), s, d))
                xn = gsb("xn3", [128, NDC * OWN], BF16)

                with contextlib.ExitStack() as lph:
                    ln = self.mk_ln(lph, tw=512)
                    for tt2 in range(2):
                        def _g(g, tt2=tt2):
                            g.wait_ge(eV.sem, eV.v)
                            if tt2 == 0:
                                g.wait_ge(dG.sem, dG.v)
                            self.dC.inc(g.dma_start(
                                out=sb3(ln["xb"][:, :], NDC),
                                in_=dram3(self.Y2[:, tt2 * 512:(tt2 + 1) * 512])))
                        blk.gpsimd(_g)
                        self.emit_ln(xn, tt2 * 512, OWN, ln, tw=512)

                self.fence(light=True)
                wgT = [gsb(f"wgT3{i}", [128, NDC * 512], BF16) for i in range(2)]
                wuT = [gsb(f"wuT3{i}", [128, NDC * 512], BF16) for i in range(2)]
                gt = gsb("gt3", [128, 512], BF16)
                slab_pe_done = {}
                prev_dve = [eV.v, eV.v]
                prev_act = [eA.v, eA.v]

                for hs in range(NHC // 4):
                    b = hs % 2

                    def _s(s, hs=hs, b=b):
                        if hs >= 2:
                            s.wait_ge(eT.sem, slab_pe_done[hs - 2])
                        for dc in range(NDC):
                            dW.inc(s.dma_start_transpose(
                                out=wgT[b][:, dc * 512:(dc + 1) * 512],
                                in_=self.WgB[hs * 512:(hs + 1) * 512,
                                             dc * 128:(dc + 1) * 128]))
                        for dc in range(NDC):
                            dW.inc(s.dma_start_transpose(
                                out=wuT[b][:, dc * 512:(dc + 1) * 512],
                                in_=self.WuB[hs * 512:(hs + 1) * 512,
                                             dc * 128:(dc + 1) * 128]))
                    blk.sync(_s)
                    slab_loaded = dW.v

                    for hcw in range(4):
                        hc = hs * 4 + hcw
                        for tt in range(2):
                            par = (hcw * 2 + tt) % 2
                            gb, ub = (4, 5) if par == 0 else (6, 7)

                            def _t(t, b=b, hcw=hcw, tt=tt, gb=gb, ub=ub, par=par,
                                   first=(hcw == 0 and tt == 0),
                                   slab_loaded=slab_loaded, hs=hs):
                                if first:
                                    t.wait_ge(dW.sem, slab_loaded)
                                    if hs == 0:
                                        t.wait_ge(eV.sem, eV.v)
                                t.wait_ge(eV.sem, prev_dve[par])
                                t.wait_ge(eA.sem, prev_act[par])
                                for dc in range(NDC):
                                    t.matmul(self.bank(gb),
                                             wgT[b][:, dc * 512 + hcw * 128:
                                                    dc * 512 + hcw * 128 + 128],
                                             xn[:, dc * OWN + tt * 512:
                                                dc * OWN + (tt + 1) * 512],
                                             start=(dc == 0), stop=(dc == NDC - 1))
                                ins = None
                                for dc in range(NDC):
                                    ins = t.matmul(self.bank(ub),
                                                   wuT[b][:, dc * 512 + hcw * 128:
                                                          dc * 512 + hcw * 128 + 128],
                                                   xn[:, dc * OWN + tt * 512:
                                                      dc * OWN + (tt + 1) * 512],
                                                   start=(dc == 0), stop=(dc == NDC - 1))
                                eT.inc(ins)
                            blk.tensor(_t)
                            slab_pe_done[hs] = eT.v

                            def _a(a, hc=hc, gb=gb):
                                a.wait_ge(eT.sem, eT.v)
                                a.wait_ge(eV.sem, eV.v)
                                eA.inc(a.activation(out=gt[:, :], in_=self.bank(gb),
                                                    func=AF.Silu,
                                                    scale=self.sgv[:, hc:hc + 1]))
                            blk.scalar(_a)
                            prev_act[par] = eA.v

                            def _v(v, hc=hc, tt=tt, ub=ub):
                                v.wait_ge(eA.sem, eA.v)
                                eV.inc(v.scalar_tensor_tensor(
                                    out=h3[:, hc * OWN + tt * 512:
                                           hc * OWN + (tt + 1) * 512],
                                    in0=self.bank(ub),
                                    scalar=self.suv[:, hc:hc + 1], in1=gt[:, :],
                                    op0=OP.mult, op1=OP.mult))
                            blk.vector(_v)
                            prev_dve[par] = eV.v

            self.fence()
            with contextlib.ExitStack() as dph:
                dsb = lambda n, s, d: dph.enter_context(nc.sbuf_tensor(self._un(n), s, d))
                wdT = dsb("wdT3", [128, NHC * 512], BF16)
                y2dn = dsb("y2dn", [128, 4 * 512], F32)
                y3t = dsb("y3t", [128, 4 * 512], F32)

                prev_down_evac = [eV.v, eV.v]
                for dq in range(4):
                    wd_tick = {}

                    def _s(s, dq=dq):
                        s.wait_ge(eT.sem, eT.v)
                        for hc in range(NHC):
                            dW.inc(s.dma_start_transpose(
                                out=wdT[:, hc * 512:(hc + 1) * 512],
                                in_=self.WdB[dq * 512:(dq + 1) * 512,
                                             hc * 128:(hc + 1) * 128]))
                            wd_tick[hc] = dW.v
                    blk.sync(_s)

                    for tt in range(2):
                        pb = (tt % 2) * 4

                        def _s(s, dq=dq, tt=tt):
                            s.wait_ge(eV.sem, eV.v)
                            dX.inc(s.dma_start(
                                out=sb3(y2dn[:, :], 4),
                                in_=dram3(self.Y2[dq * 512:(dq + 1) * 512,
                                                  tt * 512:(tt + 1) * 512])))
                        blk.sync(_s)
                        y2_loaded = dX.v

                        def _t(t, dq=dq, tt=tt, pb=pb):
                            if tt == 0 and dq == 0:
                                t.wait_ge(eV.sem, eV.v)
                            t.wait_ge(eV.sem, prev_down_evac[tt % 2])
                            ins = None
                            for hc in range(NHC):
                                if tt == 0:
                                    t.wait_ge(dW.sem, wd_tick[hc])
                                for dcw in range(4):
                                    ins = t.matmul(
                                        self.bank(pb + dcw),
                                        wdT[:, hc * 512 + dcw * 128:
                                            hc * 512 + dcw * 128 + 128],
                                        h3[:, hc * OWN + tt * 512:
                                           hc * OWN + (tt + 1) * 512],
                                        start=(hc == 0), stop=(hc == NHC - 1))
                            eT.inc(ins)
                        blk.tensor(_t)

                        def _v(v, dq=dq, tt=tt, pb=pb, y2_loaded=y2_loaded):
                            v.wait_ge(eT.sem, eT.v)
                            v.wait_ge(dX.sem, y2_loaded)
                            v.wait_ge(dG.sem, dG.v)
                            ins = None
                            for dcw in range(4):
                                dc = dq * 4 + dcw
                                ins = v.scalar_tensor_tensor(
                                    out=y3t[:, dcw * 512:(dcw + 1) * 512],
                                    in0=self.bank(pb + dcw),
                                    scalar=self.sdv[:, dc:dc + 1],
                                    in1=y2dn[:, dcw * 512:(dcw + 1) * 512],
                                    op0=OP.mult, op1=OP.add)
                            eV.inc(ins)
                            prev_down_evac[tt % 2] = eV.v
                        blk.vector(_v)

                        def _g(g, dq=dq, tt=tt):
                            g.wait_ge(eV.sem, eV.v)
                            dG.inc(g.dma_start(
                                out=dram3(self.y3T[dq * 512:(dq + 1) * 512,
                                                   tt * 512:(tt + 1) * 512]),
                                in_=sb3(y3t[:, :], 4)))
                        blk.gpsimd(_g)

    def p_final(self):
        def _g(g):
            g.wait_ge(self.dG.sem, self.dG.v)
        self.block.gpsimd(_g)

        def _s(s):
            s.wait_ge(self.dS.sem, self.dS.v)
        self.block.sync(_s)


_CACHE = {}


def get_nc(debug=False):
    key = bool(debug)
    if key not in _CACHE:
        b = Builder(debug=debug)
        _CACHE[key] = (b.build(), b)
    return _CACHE[key]


def make_inputs(inputs, debug=False):
    x = np.asarray(inputs["L0_attn_x"], dtype=np.float32)[0]
    w_qkv = np.asarray(inputs["L0_attn_w_qkv"], dtype=np.float32)
    b_qkv = np.asarray(inputs["L0_attn_b_qkv"], dtype=np.float32).reshape(3 * D, 1)
    w_out = np.asarray(inputs["L0_attn_w_out"], dtype=np.float32)
    b_out = np.asarray(inputs["L0_attn_b_out"], dtype=np.float32).reshape(D, 1)
    wq_g = np.ascontiguousarray(np.asarray(inputs["L0_ffn_wq_gate"], dtype=np.int32))
    s_g = np.asarray(inputs["L0_ffn_s_gate"], dtype=np.float32).reshape(HID, 1)
    wq_u = np.ascontiguousarray(np.asarray(inputs["L0_ffn_wq_up"], dtype=np.int32))
    s_u = np.asarray(inputs["L0_ffn_s_up"], dtype=np.float32).reshape(HID, 1)
    wq_d = np.ascontiguousarray(np.asarray(inputs["L0_ffn_wq_down"], dtype=np.int32))
    s_d = np.asarray(inputs["L0_ffn_s_down"], dtype=np.float32).reshape(D, 1)

    ident = np.eye(128, dtype=BF)
    ones = np.ones((128, 1), dtype=BF)
    o1x128 = np.ones((1, 128), dtype=BF)
    qi = np.arange(OWN)
    dmask = np.where(qi[None, :] >= np.arange(OWN)[:, None], 0.0, NEG
                     ).astype(np.float32)

    common = dict(wqg=wq_g, wqu=wq_u, wqd=wq_d, sgx=s_g, sux=s_u, sdx=s_d,
                  wqkv=np.ascontiguousarray(w_qkv), bqkv=b_qkv,
                  wout=np.ascontiguousarray(w_out), boutx=b_out,
                  diagmask=dmask, identin=ident, onesin=ones, onesk1=o1x128)

    import time
    nonce = np.array([[int(time.time_ns()) & 0x7FFFFFFF]], dtype=np.int32)
    common["noncex"] = nonce

    def member_blocks(c):
        return [c] + [b for b in range(8) if b % 2 == c % 2 and b != c]

    in_maps = []
    perms = []
    for c in range(NCORE):
        blocks = member_blocks(c)
        perms.append(blocks)
        tok = np.concatenate([np.arange(b * OWN, (b + 1) * OWN) for b in blocks])
        xTc = np.ascontiguousarray(x[tok, :].T)
        mr = np.full((1, NKC), NEG, dtype=np.float32)
        for kc in range(NKC):
            sl = kc // 32
            member = (c // 2) * 2 + sl
            blk_g = member_blocks(member)[(kc % 32) // 8]
            if blk_g < c:
                mr[0, kc] = 0.0
        m = dict(common)
        m["xT"] = xTc
        m["maskrow"] = mr
        in_maps.append(m)
    return in_maps, perms


def kernel(**inputs):
    nc, _ = get_nc(debug=False)
    in_maps, _ = make_inputs(inputs)
    res = run_bass_kernel_spmd(nc, in_maps, core_ids=list(range(NCORE)))
    out = np.empty((1, T, D), dtype=np.float32)
    for c in range(NCORE):
        out[0, c * OWN:(c + 1) * OWN, :] = res.results[c]["y3T"].T
    return out
